# version 03
# baseline (speedup 1.0000x reference)
import numpy as np
import ml_dtypes
from contextlib import ExitStack

import concourse.bass as bass
import concourse.mybir as mybir
from concourse.bass_utils import run_bass_kernel_spmd

F32 = mybir.dt.float32
BF16 = mybir.dt.bfloat16
AF = mybir.ActivationFunctionType
ALU = mybir.AluOpType
AX = mybir.AxisListType

S = 2048
D = 2048
NS = 16
DIN = 4096
HEADS = 32
HD = 128
NST = 64
DPROJ = 8352
CH = 256
NCH = 8
INTER = 1408
NEXP = 16
SINTER = 2816

DEBUG_OUT = None


class Ctr:
    def __init__(self, ctx, nc, name):
        self.h = ctx.enter_context(nc.semaphore(name))
        self.v = 0

    def inc(self, inst, n=1):
        inst.then_inc(self.h, n)
        self.v += n
        return self.v


_AUTO_OPS = {
    "activation", "add", "mul", "copy", "sign", "sqrt", "square", "memset",
    "memzero", "tensor_copy", "tensor_tensor", "tensor_scalar",
    "scalar_tensor_tensor", "tensor_reduce", "reduce_max", "reduce_sum",
    "bn_stats", "bn_aggr", "reciprocal", "tensor_tensor_scan", "select",
    "tensor_add", "tensor_mul", "tensor_sub", "tensor_max",
}


class SafeEng:

    def __init__(self, e, ctr):
        self._e = e
        self._c = ctr

    def __getattr__(self, name):
        f = getattr(self._e, name)
        if name not in _AUTO_OPS:
            return f

        def g(*a, **k):
            self._e.wait_ge(self._c.h, self._c.v)
            inst = f(*a, **k)
            self._c.inc(inst)
            return inst
        return g


class _Builder:
    def __init__(self, nc, dbg):
        self.nc = nc
        self.dbg = dbg

    def sect(self, eng):
        dec = getattr(self.block, eng)
        ctr = {"vector": None, "scalar": None}
        if eng == "vector":
            c = self.sV
        elif eng == "scalar":
            c = self.sSC
        else:
            c = None
        if c is None:
            return dec

        def mydec(f):
            def g(e):
                return f(SafeEng(e, c))
            return dec(g)
        return mydec

    def bank_wait_free(self, e, bi):
        for (ctr, v) in self.bk_rd[bi]:
            e.wait_ge(ctr.h, v)
        self.bk_rd[bi] = []

    def bank_set_read(self, bi, ctr, v):
        self.bk_rd[bi].append((ctr, v))

    def build(self):
        nc = self.nc
        dbg = self.dbg
        di = lambda n, sh: nc.dram_tensor(n, sh, F32, kind="ExternalInput").ap()
        self.u = di("u", [S, D])
        self.W_in = di("W_in", [D, DPROJ])
        self.conv_w = di("conv_w", [128, 33 * 4])
        self.conv_b = di("conv_b", [128, 33])
        self.dt_bias = di("dt_bias", [32, 1])
        self.A_log = di("A_log", [32, 1])
        self.Dh = di("Dh", [32, 1])
        self.norm_w = di("norm_w", [128, 32])
        self.W_out = di("W_out", [DIN, D])
        self.Bw = di("Bw", [1024, D])
        self.Cw = di("Cw", [D, 1024])
        self.Dab = di("Dab", [1, D])
        self.gate_w = di("gate_w", [NEXP, D])
        self.w1 = di("w1", [NEXP, INTER, D])
        self.w2 = di("w2", [NEXP, D, INTER])
        self.w3 = di("w3", [NEXP, INTER, D])
        self.sw1 = di("sw1", [SINTER, D])
        self.sw2 = di("sw2", [D, SINTER])
        self.sw3 = di("sw3", [SINTER, D])
        self.id_bf = nc.dram_tensor("id_bf", [128, 128], BF16, kind="ExternalInput").ap()
        self.ones32_bf = nc.dram_tensor("ones32_bf", [32, 128], BF16, kind="ExternalInput").ap()
        self.ones1_bf = nc.dram_tensor("ones1_bf", [1, 128], BF16, kind="ExternalInput").ap()
        self.eye32_f = nc.dram_tensor("eye32_f", [32, 32], F32, kind="ExternalInput").ap()
        self.mask_bf = nc.dram_tensor("mask_bf", [128, 512], BF16, kind="ExternalInput").ap()
        self.f_const = nc.dram_tensor("f_const", [128, 8], F32, kind="ExternalInput").ap()

        self.out = nc.dram_tensor("out", [S, D], F32, kind="ExternalOutput").ap()
        dit = lambda n, sh, dt: nc.dram_tensor(n, sh, dt, kind="Internal").ap()
        self.X1 = dit("X1", [S, D], F32)
        self.X2 = dit("X2", [S, D], F32)
        self.X3 = dit("X3", [S, D], F32)
        self.W_inB = dit("W_inB", [D, DPROJ], BF16)
        self.W_outB = dit("W_outB", [DIN, D], BF16)
        self.XT = dit("XTs", [DIN, S], BF16)
        self.ZS = dit("ZSs", [DIN, S], BF16)
        self.YG = dit("YGs", [DIN, S], BF16)
        self.w1n = dit("w1n", [NEXP, INTER, D], BF16)
        self.w2n = dit("w2n", [NEXP, D, INTER], BF16)
        self.w3n = dit("w3n", [NEXP, INTER, D], BF16)
        self.sw1n = dit("sw1n", [SINTER, D], BF16)
        self.sw2n = dit("sw2n", [D, SINTER], BF16)
        self.sw3n = dit("sw3n", [SINTER, D], BF16)
        self.Bn = dit("Bn", [1024, D], BF16)
        self.Cn = dit("Cn", [D, 1024], BF16)
        self.w1t = dit("w1t", [NEXP, 11, 128, 16, 128], BF16)
        self.w2t = dit("w2t", [NEXP, 4, 128, 11, 512], BF16)
        self.w3t = dit("w3t", [NEXP, 11, 128, 16, 128], BF16)
        self.sw1t = dit("sw1t", [22, 128, 16, 128], BF16)
        self.sw2t = dit("sw2t", [4, 128, 22, 512], BF16)
        self.sw3t = dit("sw3t", [22, 128, 16, 128], BF16)
        self.Bt = dit("Bt", [D, 1024], BF16)
        self.Ct = dit("Ct", [1024, D], BF16)

        if dbg:
            shp = {"t1T": [128, NS * S], "zs": [DIN, S], "xt": [DIN, S], "dtv": [32, S],
                   "x1": [S, D], "yg": [DIN, S], "x2": [S, D], "x3": [S, D]}[dbg]
            dt_ = BF16 if dbg in ("t1T", "zs", "xt", "yg") else F32
            self.dbg_t = nc.dram_tensor("dbg", shp, dt_, kind="ExternalOutput").ap()

        with ExitStack() as ctx:
            E = ctx.enter_context
            self.ctx = ctx
            self.c_id = E(nc.sbuf_tensor([128, 128], BF16))
            self.c_ones32 = E(nc.sbuf_tensor([32, 128], BF16))
            self.c_ones1 = E(nc.sbuf_tensor([1, 128], BF16))
            self.c_onesc = E(nc.sbuf_tensor([128, 1], BF16))
            self.c_eye32 = E(nc.sbuf_tensor([32, 32], F32))
            self.c_mask = E(nc.sbuf_tensor([128, 512], BF16))
            self.c_f = E(nc.sbuf_tensor([128, 8], F32))
            self.ones_a = E(nc.sbuf_tensor([32, CH], F32))
            self.csel_hi = E(nc.sbuf_tensor([32, 32], BF16))
            self.csel_lo = E(nc.sbuf_tensor([32, 32], BF16))
            self.eyeD = E(nc.sbuf_tensor([32, 32], F32))
            self.eyeD_bf = E(nc.sbuf_tensor([32, 32], BF16))
            self.gate_tmp = E(nc.sbuf_tensor([128, 16], F32))
            self.gate_tmp2 = E(nc.sbuf_tensor([128, 16], F32))
            self.gate_col = E(nc.sbuf_tensor([128, 4], F32))
            self.gwT_hi = E(nc.sbuf_tensor([128, 16 * 16], BF16))
            self.gwT_lo = E(nc.sbuf_tensor([128, 16 * 16], BF16))
            self.ytmp = E(nc.sbuf_tensor([128, CH], F32))
            self.bd = E(nc.sbuf_tensor([128, 4 * NST], BF16))
            self.krow = E(nc.sbuf_tensor([128, 2 * SINTER], BF16))

            self.qS = Ctr(ctx, nc, "qS")
            self.qA = Ctr(ctx, nc, "qA")
            self.qG = Ctr(ctx, nc, "qG")
            self.sPE = Ctr(ctx, nc, "sPE")
            self.sV = Ctr(ctx, nc, "sV")
            self.sSC = Ctr(ctx, nc, "sSC")
            self.block = E(nc.Block())
            self.bkt = [E(nc.psum_tensor(f"bank{i}", [128, 512], F32)) for i in range(6)]
            self.bk_rd = [[] for _ in range(6)]
            self.psT = [E(nc.psum_tensor(f"psT{i}", [128, 256], BF16)) for i in range(2)]
            self.psT_rd = [[], []]

            self._load_consts()
            self._emit_prepass_casts()
            self.prepass_gen = self._prepass_transpose_gen()
            self.prepass_left = True

            self._mamba()
            if dbg in ("t1T", "zs", "xt", "dtv", "yg", "x1"):
                self._finish()
                return
            self._gate_prep()
            self._ffn(self.X1, self.X2, self.x_store, 0)
            if dbg == "x2":
                @self.sect("sync")
                def _(e):
                    for (ctr, v) in self.x_store:
                        e.wait_ge(ctr.h, v)
                    self.qS.inc(e.dma_start(self.dbg_t, self.X2), 16)
                self.out_done = [(self.qS, self.qS.v)]
                self._finish()
                return
            self._attnb()
            if dbg == "x3":
                self._finish()
                return
            self._ffn(self.X3, self.out, self.x_store, 1)
            self._finish()

    def _bank(self, i):
        return self.bkt[i]

    def _load_consts(self):
        q = self.qS

        @self.sect("sync")
        def _(e):
            q.inc(e.dma_start(self.c_id[:], self.id_bf), 16)
            q.inc(e.dma_start(self.c_ones32[:], self.ones32_bf), 16)
            q.inc(e.dma_start(self.c_ones1[:], self.ones1_bf), 16)
            q.inc(e.dma_start(self.c_eye32[:], self.eye32_f), 16)
            q.inc(e.dma_start(self.c_mask[:], self.mask_bf), 16)
            q.inc(e.dma_start(self.c_f[:], self.f_const), 16)
        self.consts_done = q.v

        @self.sect("vector")
        def _(e):
            e.memset(self.ones_a[:], 1.0)
            e.memset(self.c_onesc[:], 1.0)
        self.consts_v = self.sV.v

    def _emit_prepass_casts(self):
        qG = self.qG
        self.cast_done = {}

        @self.sect("gpsimd")
        def _(e):
            qG.inc(e.dma_start(self.W_inB, self.W_in), 16)
            self.cast_done["W_in"] = qG.v
            for ei in range(NEXP):
                qG.inc(e.dma_start(self.w1n[ei], self.w1[ei]), 16)
                self.cast_done[f"w1.{ei}"] = qG.v
                qG.inc(e.dma_start(self.w3n[ei], self.w3[ei]), 16)
                self.cast_done[f"w3.{ei}"] = qG.v
                qG.inc(e.dma_start(self.w2n[ei], self.w2[ei]), 16)
                self.cast_done[f"w2.{ei}"] = qG.v
                if ei == 1:
                    qG.inc(e.dma_start(self.W_outB, self.W_out), 16)
                    self.cast_done["W_out"] = qG.v
            qG.inc(e.dma_start(self.sw1n, self.sw1), 16)
            self.cast_done["sw1"] = qG.v
            qG.inc(e.dma_start(self.sw3n, self.sw3), 16)
            self.cast_done["sw3"] = qG.v
            qG.inc(e.dma_start(self.sw2n, self.sw2), 16)
            self.cast_done["sw2"] = qG.v
            qG.inc(e.dma_start(self.Bn, self.Bw), 16)
            self.cast_done["B"] = qG.v
            qG.inc(e.dma_start(self.Cn, self.Cw), 16)
            self.cast_done["C"] = qG.v

    def _prepass_transpose_gen(self):
        jobs = []
        for ei in range(NEXP):
            jobs.append((f"w1.{ei}", self.w1n[ei], self.w1t[ei], INTER, D, "i128"))
            jobs.append((f"w3.{ei}", self.w3n[ei], self.w3t[ei], INTER, D, "i128"))
            jobs.append((f"w2.{ei}", self.w2n[ei], self.w2t[ei], D, INTER, "nb512"))
        jobs.append(("sw1", self.sw1n, self.sw1t, SINTER, D, "i128"))
        jobs.append(("sw3", self.sw3n, self.sw3t, SINTER, D, "i128"))
        jobs.append(("sw2", self.sw2n, self.sw2t, D, SINTER, "nb512"))
        jobs.append(("B", self.Bn, self.Bt, 1024, D, "plain"))
        jobs.append(("C", self.Cn, self.Ct, D, 1024, "plain"))
        self.tp_done = {}
        slot = 0
        slot_store_v = [0, 0]
        qS, qG = self.qS, self.qG
        for name, nat, tp, O, I, kind in jobs:
            castv = self.cast_done[name]
            for k in range(I // 128):
                kb = self.krow[:, slot * SINTER: slot * SINTER + O]
                sv = slot_store_v[slot]

                @self.sect("sync")
                def _(e, kb=kb, nat=nat, k=k, castv=castv, sv=sv):
                    e.wait_ge(qG.h, max(castv, sv))
                    qS.inc(e.dma_start(kb, nat[:, k * 128:(k + 1) * 128], transpose=True), 16)
                loadv = qS.v

                @self.sect("gpsimd")
                def _(e, kb=kb, tp=tp, k=k, loadv=loadv, kind=kind):
                    e.wait_ge(qS.h, loadv)
                    if kind == "i128":
                        dst = tp[:, :, k, :].rearrange("i p j -> p i j")
                        src = kb.rearrange("p (i j) -> p i j", j=128)
                    elif kind == "nb512":
                        dst = tp[:, :, k, :].rearrange("nb p j -> p nb j")
                        src = kb.rearrange("p (nb j) -> p nb j", j=512)
                    else:
                        dst = tp[k * 128:(k + 1) * 128, :]
                        src = kb
                    qG.inc(e.dma_start(dst, src), 16)
                slot_store_v[slot] = qG.v
                slot = (slot + 1) % 2
                yield
            self.tp_done[name] = (qG, qG.v)
        self.prepass_left = False
        while True:
            yield

    def pump(self, n):
        if not self.prepass_left:
            return
        for _ in range(n):
            next(self.prepass_gen)
            if not self.prepass_left:
                break

    def wait_tp(self, e, name):
        while self.prepass_left and name not in self.tp_done:
            next(self.prepass_gen)
        ctr, v = self.tp_done[name]
        e.wait_ge(ctr.h, v)

    def _pe_transpose(self, src_ap, w, dst_ap, dst_is_add_pair=False, first_waits=None):
        pslot = self.psT_slot
        self.psT_slot ^= 1
        ps = self.psT[pslot]
        npart = (src_ap[0] if dst_is_add_pair else src_ap).shape[0]
        ident = self.c_id[0:npart, 0:npart]

        @self.sect("tensor")
        def _(e):
            if first_waits:
                for (ctr, v) in first_waits:
                    e.wait_ge(ctr.h, v)
            for (ctr, v) in self.psT_rd[pslot]:
                e.wait_ge(ctr.h, v)
            self.psT_rd[pslot] = []
            if dst_is_add_pair:
                e.transpose(ps[0:w, 0:npart], src_ap[0], ident)
                self.sPE.inc(e.transpose(ps[0:w, npart:2 * npart], src_ap[1], ident))
            else:
                self.sPE.inc(e.transpose(ps[0:w, 0:npart], src_ap, ident))
        pev = self.sPE.v

        @self.sect("vector")
        def _(e):
            e.wait_ge(self.sPE.h, pev)
            if dst_is_add_pair:
                e.tensor_copy(dst_ap, ps[0:w, 0:npart])
                e.tensor_tensor(dst_ap, dst_ap, ps[0:w, npart:2 * npart], ALU.add)
            else:
                e.tensor_copy(dst_ap, ps[0:w, 0:npart])
            self.psT_rd[pslot].append((self.sV, self.sV.v))
        return self.sV.v

    _norm_seq = [0]

    def _norm_t(self, pctx, X, tT, tile_ready=None, gate=False, save_stats=None):
        nc = self.nc
        E = pctx.enter_context
        self._norm_seq[0] += 1
        nsq = self._norm_seq[0]
        xa = E(nc.sbuf_tensor([128, 2 * D], F32))
        bn = E(nc.sbuf_tensor([128, 24], F32))
        st = E(nc.sbuf_tensor([128, 8], F32))
        tf = E(nc.sbuf_tensor(f"nt_tf{nsq}", [128, D], F32)) if gate else None
        tb = E(nc.sbuf_tensor([128, D], BF16))
        tlo = E(nc.sbuf_tensor(f"nt_tlo{nsq}", [128, D], BF16)) if gate else None
        tloT = E(nc.sbuf_tensor(f"nt_tloT{nsq}", [128, 16 * 128], BF16)) if gate else None
        act_done = [0] * NS
        gate_done = []

        for i in range(NS):
            slot = i % 2
            xt = xa[:, slot * D:(slot + 1) * D]

            @self.sect("sync")
            def _(e, i=i, xt=xt, slot=slot):
                if i >= 2:
                    e.wait_ge(self.sV.h, act_done[i - 2])
                if tile_ready is not None:
                    if i == 0:
                        for (ctr, v) in tile_ready:
                            e.wait_ge(ctr.h, v)
                    else:
                        ctr, v = tile_ready[i]
                        e.wait_ge(ctr.h, v)
                self.qS.inc(e.dma_start(xt, X[i * 128:(i + 1) * 128, :]), 16)
            loadv = self.qS.v

            @self.sect("vector")
            def _(e, xt=xt, loadv=loadv):
                e.wait_ge(self.qS.h, loadv)
                for j in range(4):
                    e.bn_stats(bn[:, j * 6:(j + 1) * 6], xt[:, j * 512:(j + 1) * 512])
                e.bn_aggr(st[:, 0:2], bn[:])
            aggv = self.sV.v

            @self.sect("scalar")
            def _(e, aggv=aggv):
                e.wait_ge(self.sV.h, aggv)
                e.activation(st[:, 4:5], st[:, 1:2], AF.Ln, bias=self.c_f[:, 0:1])
                e.activation(st[:, 2:3], st[:, 4:5], AF.Exp, scale=self.c_f[:, 1:2])
            rstdv = self.sSC.v

            @self.sect("vector")
            def _(e, rstdv=rstdv):
                e.wait_ge(self.sSC.h, rstdv)
                e.tensor_scalar(st[:, 3:4], st[:, 0:1], st[:, 2:3], -1.0, ALU.mult, ALU.mult)
            mbv = self.sV.v

            if save_stats is not None:
                @self.sect("scalar")
                def _(e, mbv=mbv, i=i):
                    e.wait_ge(self.sV.h, mbv)
                    rs_all, mb_all = save_stats
                    e.copy(rs_all[:, i:i + 1], st[:, 2:3])
                    e.copy(mb_all[:, i:i + 1], st[:, 3:4])

            @self.sect("vector")
            def _(e, xt=xt, i=i):
                if gate:
                    e.tensor_scalar(tf[:], xt, st[:, 2:3], st[:, 3:4], ALU.mult, ALU.add)
                    e.tensor_copy(tb[:], tf[:])
                    e.tensor_tensor(tlo[:], tf[:], tb[:], ALU.subtract)
                else:
                    e.tensor_scalar(tb[:], xt, st[:, 2:3], st[:, 3:4], ALU.mult, ALU.add)
            actv = self.sV.v
            act_done[i] = actv
            tlov = actv

            fw = [(self.sV, actv), (self.qS, self.consts_done)]
            for k in range(16):
                self._pe_transpose(tb[:, k * 128:(k + 1) * 128], 128,
                                   tT[:, k * S + i * 128: k * S + (i + 1) * 128],
                                   first_waits=fw if k == 0 else None)
                if gate:
                    self._pe_transpose(tlo[:, k * 128:(k + 1) * 128], 128,
                                       tloT[:, k * 128:(k + 1) * 128],
                                       first_waits=[(self.sV, tlov)] if k == 0 else None)
            tpv = self.sV.v

            if gate:
                bi = 0
                bank = self._bank(bi)

                @self.sect("tensor")
                def _(e, i=i, tpv=tpv, bank=bank, bi=bi):
                    e.wait_ge(self.sV.h, tpv)
                    self.bank_wait_free(e, bi)
                    nmm = 48
                    j = 0
                    for k in range(16):
                        tTk = tT[:, k * S + i * 128: k * S + (i + 1) * 128]
                        tlk = tloT[:, k * 128:(k + 1) * 128]
                        for (l, r) in ((tTk, self.gwT_hi[:, k * 16:(k + 1) * 16]),
                                       (tTk, self.gwT_lo[:, k * 16:(k + 1) * 16]),
                                       (tlk, self.gwT_hi[:, k * 16:(k + 1) * 16])):
                            inst = e.matmul(bank[:, 0:16], l, r, start=(j == 0), stop=(j == nmm - 1))
                            j += 1
                    self.sPE.inc(inst)
                gmv = self.sPE.v
                self._gate_softmax(i, bi, gmv)
                gate_done.append(self.sV.v)
        return act_done

    def _gate_softmax(self, i, bi, gmv):
        g = self.gate_tmp
        g2 = self.gate_tmp2
        col = self.gate_col

        @self.sect("vector")
        def _(e):
            e.wait_ge(self.sPE.h, gmv)
            e.tensor_copy(g[:], self._bank(bi)[:, 0:16])
            e.tensor_reduce(col[:, 0:1], g[:], AX.X, ALU.max)
            self.bank_set_read(bi, self.sV, self.sV.v)
        v1 = self.sV.v

        @self.sect("scalar")
        def _(e):
            e.wait_ge(self.sV.h, v1)
            e.mul(col[:, 1:2], col[:, 0:1], -1.0)
        negv = self.sSC.v

        @self.sect("scalar")
        def _(e):
            e.wait_ge(self.sSC.h, negv)
            e.activation(g[:], g[:], AF.Exp, bias=col[:, 1:2])
        ev = self.sSC.v

        @self.sect("vector")
        def _(e, i=i):
            e.wait_ge(self.sSC.h, ev)
            e.tensor_reduce(col[:, 2:3], g[:], AX.X, ALU.add)
            e.reciprocal(col[:, 2:3], col[:, 2:3])
            e.tensor_scalar(g[:], g[:], col[:, 2:3], None, ALU.mult)
            e.tensor_reduce(col[:, 0:1], g[:], AX.X, ALU.max)
            e.tensor_scalar(g2[:], g[:], col[:, 0:1], None, ALU.is_lt)
            e.tensor_tensor(g2[:], g2[:], g[:], ALU.mult)
            e.tensor_reduce(col[:, 3:4], g2[:], AX.X, ALU.max)
            e.tensor_scalar(g2[:], g[:], col[:, 3:4], None, ALU.is_ge)
            e.tensor_tensor(self.wg[:, i * 16:(i + 1) * 16], g2[:], g[:], ALU.mult)

    def _mamba(self):
        nc = self.nc
        self.psT_slot = 0
        with ExitStack() as pers:
            PE = pers.enter_context
            self.bct = PE(nc.sbuf_tensor([128, S], BF16))
            self.A_dt = PE(nc.sbuf_tensor([32, S], F32))
            self.dtT = PE(nc.sbuf_tensor([128, NS * 32], F32))
            self.cDh = PE(nc.sbuf_tensor([32, 1], F32))
            self.Dcol = PE(nc.sbuf_tensor([128, 32], F32))
            self.nw = PE(nc.sbuf_tensor([128, 32], F32))
            with ExitStack() as pctx:
                E = pctx.enter_context
                t1T = E(nc.sbuf_tensor([128, NS * S], BF16))
                act_done = self._norm_t(pctx, self.u, t1T)
                t1T_done = self.sV.v

                if self.dbg == "t1T":
                    @self.sect("sync")
                    def _(e):
                        e.wait_ge(self.sV.h, t1T_done)
                        self.qS.inc(e.dma_start(self.dbg_t, t1T[:]), 16)
                    self.out_done = [(self.qS, self.qS.v)]
                    return

                self._w_in_pass(pctx, t1T, t1T_done)
                if self.dbg in ("zs", "xt", "dtv"):
                    return
            with ExitStack() as p2:
                self._ssd(p2)
                if self.dbg == "yg":
                    return
        self._w_out_pass()

    def _w_in_pass(self, pctx, t1T, t1T_done):
        nc = self.nc
        E = pctx.enter_context
        wf = E(nc.sbuf_tensor([128, 2 * 16 * 512], BF16))
        xpad = E(nc.sbuf_tensor([128, 3 + S], F32))
        acc0 = E(nc.sbuf_tensor([128, S], F32))
        acc1 = E(nc.sbuf_tensor([128, S], F32))
        row = E(nc.sbuf_tensor([128, S], BF16))
        cw = E(nc.sbuf_tensor([128, 33 * 4], F32))
        cb = E(nc.sbuf_tensor([128, 33], F32))
        dtraw = E(nc.sbuf_tensor([32, S], F32))
        dthb = E(nc.sbuf_tensor([32, S], BF16))
        dtlb = E(nc.sbuf_tensor([32, S], BF16))
        smallv = E(nc.sbuf_tensor([32, 4], F32))
        self.dt_sp = E(nc.sbuf_tensor([32, S], F32))

        @self.sect("sync")
        def _(e):
            self.qS.inc(e.dma_start(cw[:], self.conv_w), 16)
            self.qS.inc(e.dma_start(cb[:], self.conv_b), 16)
            self.qS.inc(e.dma_start(smallv[:, 0:1], self.dt_bias), 16)
            self.qS.inc(e.dma_start(smallv[:, 1:2], self.A_log), 16)
            self.qS.inc(e.dma_start(self.cDh[:], self.Dh), 16)
            self.qS.inc(e.dma_start(self.nw[:], self.norm_w), 16)
        sm_load = self.qS.v

        @self.sect("scalar")
        def _(e):
            e.wait_ge(self.qS.h, sm_load)
            e.activation(smallv[:, 2:3], smallv[:, 1:2], AF.Exp)
            e.mul(smallv[:, 2:3], smallv[:, 2:3], -1.0)
        expA_done = self.sSC.v

        @self.sect("vector")
        def _(e):
            e.wait_ge(self.qS.h, sm_load)
            e.wait_ge(self.sV.h, self.consts_v)
            e.tensor_scalar(self.eyeD[:], self.c_eye32[:], self.cDh[:], None, ALU.mult)
            e.tensor_copy(self.eyeD_bf[:], self.eyeD[:])
            e.tensor_tensor(self.eyeD[:], self.eyeD[:], self.eyeD_bf[:], ALU.subtract)
            e.tensor_copy(self.csel_lo[:], self.eyeD[:])
        dselv = self.sV.v
        bi = 0

        @self.sect("tensor")
        def _(e):
            e.wait_ge(self.sV.h, dselv)
            e.wait_ge(self.qS.h, self.consts_done)
            self.bank_wait_free(e, bi)
            e.matmul(self._bank(bi)[:, 0:32], self.c_ones32[:], self.eyeD_bf[:], start=True, stop=False)
            self.sPE.inc(e.matmul(self._bank(bi)[:, 0:32], self.c_ones32[:], self.csel_lo[:], start=False, stop=True))
        dmmv = self.sPE.v

        @self.sect("vector")
        def _(e):
            e.wait_ge(self.sPE.h, dmmv)
            e.tensor_copy(self.Dcol[:], self._bank(bi)[:, 0:32])
            self.bank_set_read(bi, self.sV, self.sV.v)
        self.Dcol_done = self.sV.v

        fblocks = [(f, 512) for f in range(0, 8192, 512)] + [(8192, 160)]
        mm_done_for_slot = [0, 0]
        row_store_q = 0
        self.dtraw_done = 0
        self.bct_done = 0

        def emit_load(fbi):
            f0, fbw = fblocks[fbi]
            slot = fbi % 2
            wfs = wf[:, slot * 16 * 512: slot * 16 * 512 + 16 * fbw]

            @self.sect("gpsimd")
            def _(e, f0=f0, fbw=fbw, wfs=wfs, slot=slot, fbi=fbi):
                e.wait_ge(self.qG.h, self.cast_done["W_in"])
                if fbi >= 2:
                    e.wait_ge(self.sPE.h, mm_done_for_slot[slot])
                src = self.W_inB[:, f0:f0 + fbw].rearrange("(k p) j -> p k j", p=128)
                self.qG.inc(e.dma_start(wfs.rearrange("p (k j) -> p k j", k=16), src), 16)
            return wfs, self.qG.v

        loads = {0: emit_load(0), 1: emit_load(1)}
        for fbi, (f0, fbw) in enumerate(fblocks):
            wfs, wf_load = loads.pop(fbi)
            slot = fbi % 2
            nm = (fbw + 127) // 128
            for m in range(nm):
                mw = min(128, fbw - m * 128)
                ftile = f0 + m * 128
                for nb in range(4):
                    bi2 = nb
                    bank = self._bank(bi2)
                    first = (m == 0 and nb == 0)

                    @self.sect("tensor")
                    def _(e, wfs=wfs, m=m, nb=nb, mw=mw, fbw=fbw, bi2=bi2, wf_load=wf_load, first=first):
                        if first:
                            e.wait_ge(self.qG.h, wf_load)
                            e.wait_ge(self.sV.h, t1T_done)
                        self.bank_wait_free(e, bi2)
                        for k in range(16):
                            inst = e.matmul(self._bank(bi2)[0:mw, :],
                                            wfs[:, k * fbw + m * 128: k * fbw + m * 128 + mw],
                                            t1T[:, k * S + nb * 512: k * S + (nb + 1) * 512],
                                            start=(k == 0), stop=(k == 15))
                        self.sPE.inc(inst)
                    mmv = self.sPE.v
                    mm_done_for_slot[slot] = mmv

                    if ftile < 4096:
                        @self.sect("scalar")
                        def _(e, nb=nb, bi2=bi2, mmv=mmv, rsq=row_store_q):
                            e.wait_ge(self.sPE.h, mmv)
                            if nb == 0:
                                e.wait_ge(self.qA.h, rsq)
                            e.activation(row[:, nb * 512:(nb + 1) * 512],
                                                      self._bank(bi2)[:], AF.Silu)
                            self.bank_set_read(bi2, self.sSC, self.sSC.v)
                        if nb == 3:
                            ev = self.sSC.v

                            @self.sect("scalar")
                            def _(e, ftile=ftile, ev=ev):
                                e.wait_ge(self.sSC.h, ev)
                                self.qA.inc(e.dma_start(self.ZS[ftile:ftile + 128, :], row[:]), 16)
                            row_store_q = self.qA.v
                    elif ftile < 8320:
                        @self.sect("vector")
                        def _(e, nb=nb, bi2=bi2, mmv=mmv):
                            e.wait_ge(self.sPE.h, mmv)
                            e.tensor_copy(xpad[:, 3 + nb * 512: 3 + (nb + 1) * 512],
                                                      self._bank(bi2)[:])
                            self.bank_set_read(bi2, self.sV, self.sV.v)
                        if nb == 3:
                            xt_i = (ftile - 4096) // 128
                            cpv = self.sV.v

                            @self.sect("vector")
                            def _(e, xt_i=xt_i, cpv=cpv):
                                e.wait_ge(self.sV.h, cpv)
                                if xt_i == 0:
                                    e.memset(xpad[:, 0:3], 0.0)
                                e.tensor_scalar(acc0[:], xpad[:, 0:S], cw[:, xt_i * 4: xt_i * 4 + 1], None, ALU.mult)
                                e.scalar_tensor_tensor(acc1[:], xpad[:, 1:1 + S], cw[:, xt_i * 4 + 1: xt_i * 4 + 2], acc0[:], ALU.mult, ALU.add)
                                e.scalar_tensor_tensor(acc0[:], xpad[:, 2:2 + S], cw[:, xt_i * 4 + 2: xt_i * 4 + 3], acc1[:], ALU.mult, ALU.add)
                                e.scalar_tensor_tensor(acc1[:], xpad[:, 3:3 + S], cw[:, xt_i * 4 + 3: xt_i * 4 + 4], acc0[:], ALU.mult, ALU.add)
                            convv = self.sV.v

                            @self.sect("scalar")
                            def _(e, xt_i=xt_i, convv=convv, rsq=row_store_q):
                                e.wait_ge(self.sV.h, convv)
                                if xt_i < 32:
                                    e.wait_ge(self.qA.h, rsq)
                                    dst = row[:]
                                else:
                                    dst = self.bct[:]
                                e.activation(dst, acc1[:], AF.Silu, bias=cb[:, xt_i:xt_i + 1])
                            sv = self.sSC.v
                            if xt_i < 32:
                                @self.sect("scalar")
                                def _(e, xt_i=xt_i, sv=sv):
                                    e.wait_ge(self.sSC.h, sv)
                                    self.qA.inc(e.dma_start(self.XT[xt_i * 128:(xt_i + 1) * 128, :], row[:]), 16)
                                row_store_q = self.qA.v
                            else:
                                self.bct_done = sv
                    else:
                        @self.sect("vector")
                        def _(e, nb=nb, bi2=bi2, mmv=mmv):
                            e.wait_ge(self.sPE.h, mmv)
                            e.tensor_copy(dtraw[:, nb * 512:(nb + 1) * 512], self._bank(bi2)[0:32, :])
                            self.bank_set_read(bi2, self.sV, self.sV.v)
                        if nb == 3:
                            self.dtraw_done = self.sV.v
                self.pump(1)
            if fbi + 2 < len(fblocks):
                loads[fbi + 2] = emit_load(fbi + 2)
        self.xt_zs_store = self.qA.v

        @self.sect("vector")
        def _(e):
            e.wait_ge(self.sV.h, self.dtraw_done)
            e.wait_ge(self.sSC.h, expA_done)
            e.tensor_scalar(dtraw[:], dtraw[:], smallv[:, 0:1], None, ALU.add)
        t1v = self.sV.v

        @self.sect("scalar")
        def _(e):
            e.wait_ge(self.sV.h, t1v)
            e.activation(self.dt_sp[:], dtraw[:], AF.Abs)
            e.activation(self.dt_sp[:], self.dt_sp[:], AF.Exp, scale=self.c_f[0:32, 2:3])
            e.activation(self.dt_sp[:], self.dt_sp[:], AF.Ln, bias=self.c_f[0:32, 3:4])
        t2v = self.sSC.v

        @self.sect("vector")
        def _(e):
            e.wait_ge(self.sSC.h, t2v)
            e.tensor_scalar(self.A_dt[:], dtraw[:], 0.0, None, ALU.max)
            e.tensor_tensor(self.dt_sp[:], self.dt_sp[:], self.A_dt[:], ALU.add)
            e.tensor_scalar(self.A_dt[:], self.dt_sp[:], smallv[:, 2:3], None, ALU.mult)
            e.tensor_copy(dthb[:], self.dt_sp[:])
            e.tensor_tensor(dtlb[:], self.dt_sp[:], dthb[:], ALU.subtract)
        self.dt_done = self.sV.v

        for i in range(NS):
            self._pe_transpose((dthb[:, i * 128:(i + 1) * 128], dtlb[:, i * 128:(i + 1) * 128]), 128,
                               self.dtT[:, i * 32:(i + 1) * 32], dst_is_add_pair=True,
                               first_waits=[(self.sV, self.dt_done)] if i == 0 else None)
        self.dtT_done = self.sV.v

        if self.dbg == "zs":
            self._dump_scratch(self.ZS)
        if self.dbg == "xt":
            self._dump_scratch(self.XT)
        if self.dbg == "dtv":
            @self.sect("sync")
            def _(e):
                e.wait_ge(self.sV.h, self.dt_done)
                self.qS.inc(e.dma_start(self.dbg_t, self.dt_sp[:]), 16)
            self.out_done = [(self.qS, self.qS.v)]

    def _dump_scratch(self, T):
        @self.sect("sync")
        def _(e):
            e.wait_ge(self.qA.h, self.qA.v)
            self.qS.inc(e.dma_start(self.dbg_t, T), 16)
        self.out_done = [(self.qS, self.qS.v)]

    def _ssd(self, pctx):
        nc = self.nc
        E = pctx.enter_context
        xtc = E(nc.sbuf_tensor([128, 32 * CH], BF16))
        xs = E(nc.sbuf_tensor([128, 2 * DIN], BF16))
        zsc = E(nc.sbuf_tensor([128, 32 * CH], BF16))
        acs = E(nc.sbuf_tensor([32, CH], F32))
        acsT = E(nc.sbuf_tensor([128, 2 * 32], F32))
        ecsP = E(nc.sbuf_tensor([128, 32], F32))
        decT = E(nc.sbuf_tensor([128, 2 * 32], F32))
        selq = E(nc.sbuf_tensor([32, 2 * 2 * CH], BF16))
        dq = E(nc.sbuf_tensor([128, 2 * 2 * CH], F32))
        eq = E(nc.sbuf_tensor([128, 2 * 2 * CH], F32))
        Mq = E(nc.sbuf_tensor([128, 2 * 2 * CH], BF16))
        xsc = E(nc.sbuf_tensor([128, 2 * DIN], BF16))
        btok = E(nc.sbuf_tensor([128, 2 * NST], BF16))
        cc0 = E(nc.sbuf_tensor([64, CH], BF16))
        Ce = E(nc.sbuf_tensor([64, 2 * CH], BF16))
        eacs = E(nc.sbuf_tensor([64, 2 * CH], F32))
        Gs = E(nc.sbuf_tensor([128, 2 * CH], F32))
        Gm = E(nc.sbuf_tensor([128, 2 * CH], F32))
        acshl = E(nc.sbuf_tensor([32, 2 * CH], BF16))
        Sst = E(nc.sbuf_tensor([64, HEADS * HD], F32))
        Sbf = E(nc.sbuf_tensor([64, HEADS * HD], BF16))
        yg = E(nc.sbuf_tensor([128, 32 * CH], F32))
        ygb = E(nc.sbuf_tensor([128, 32 * CH], BF16))
        sq = ygb
        rr = E(nc.sbuf_tensor([1, 2 * CH], F32))
        rstd_hi = E(nc.sbuf_tensor([1, CH], BF16))
        rstd_lo = E(nc.sbuf_tensor([1, CH], BF16))
        rbc = E(nc.sbuf_tensor([128, CH], F32))

        @self.sect("vector")
        def _(e):
            e.wait_ge(self.qA.h, self.xt_zs_store)
            e.memset(Sst[:], 0.0)
            e.memset(Sbf[:], 0.0)
        sbf_ready = [self.sV.v] * HEADS

        self.yg_store = 0
        prev_chunk_pe = 0
        for c in range(NCH):
            all_v_now = self.sV.v

            @self.sect("sync")
            def _(e, c=c, all_v_now=all_v_now):
                e.wait_ge(self.sV.h, all_v_now if c >= 1 else self.dtT_done)
                e.wait_ge(self.qA.h, self.xt_zs_store)
                src = self.XT[:, c * CH:(c + 1) * CH].rearrange("(k p) j -> p k j", p=128)
                self.qS.inc(e.dma_start(xtc.ap().rearrange("p (k j) -> p k j", k=32), src), 16)
                src2 = self.ZS[:, c * CH:(c + 1) * CH].rearrange("(k p) j -> p k j", p=128)
                self.qS.inc(e.dma_start(zsc.ap().rearrange("p (k j) -> p k j", k=32), src2), 16)
                for qh in range(2):
                    self.qS.inc(e.dma_start(xs[:, qh * DIN:(qh + 1) * DIN],
                                            self.XT[:, c * CH + qh * 128: c * CH + (qh + 1) * 128],
                                            transpose=True), 16)
            ld_v = self.qS.v

            @self.sect("vector")
            def _(e, c=c, prev_chunk_pe=prev_chunk_pe):
                e.wait_ge(self.sV.h, self.dt_done)
                if c >= 1:
                    e.wait_ge(self.sPE.h, prev_chunk_pe)
                e.tensor_tensor_scan(acs[:], self.ones_a[:], self.A_dt[:, c * CH:(c + 1) * CH], 0.0, ALU.mult, ALU.add)
                e.tensor_copy(acshl[:, 0:CH], acs[:])
                e.tensor_tensor(acshl[:, CH:2 * CH], acs[:], acshl[:, 0:CH], ALU.subtract)
            acs_v = self.sV.v

            for qh in range(2):
                self._pe_transpose((acshl[:, qh * 128:(qh + 1) * 128], acshl[:, CH + qh * 128: CH + (qh + 1) * 128]), 128,
                                   acsT[:, qh * 32:(qh + 1) * 32],
                                   dst_is_add_pair=True,
                                   first_waits=[(self.sV, acs_v)])
            acsT_v = self.sV.v

            @self.sect("vector")
            def _(e):
                e.tensor_scalar(self.eyeD[:], self.c_eye32[:], acs[:, CH - 1:CH], None, ALU.mult)
                e.tensor_copy(self.csel_hi[:], self.eyeD[:])
                e.tensor_tensor(self.eyeD[:], self.eyeD[:], self.csel_hi[:], ALU.subtract)
                e.tensor_copy(self.csel_lo[:], self.eyeD[:])
            csel_v = self.sV.v
            bank0 = self._bank(0)

            @self.sect("tensor")
            def _(e, csel_v=csel_v):
                e.wait_ge(self.sV.h, csel_v)
                self.bank_wait_free(e, 0)
                e.matmul(bank0[:, 0:32], self.c_ones32[:], self.csel_hi[:], start=True, stop=False)
                self.sPE.inc(e.matmul(bank0[:, 0:32], self.c_ones32[:], self.csel_lo[:], start=False, stop=True))
            csmm_v = self.sPE.v

            @self.sect("scalar")
            def _(e, csmm_v=csmm_v):
                e.wait_ge(self.sPE.h, csmm_v)
                e.activation(ecsP[:], bank0[:, 0:32], AF.Exp)
            ecs_v = self.sSC.v

            @self.sect("vector")
            def _(e, csmm_v=csmm_v):
                e.wait_ge(self.sPE.h, csmm_v)
                e.tensor_tensor(decT[:, 0:32], bank0[:, 0:32], acsT[:, 0:32], ALU.subtract)
                e.tensor_tensor(decT[:, 32:64], bank0[:, 0:32], acsT[:, 32:64], ALU.subtract)
                self.bank_set_read(0, self.sV, self.sV.v)
                self.bank_set_read(0, self.sSC, ecs_v)
            dsub_v = self.sV.v

            @self.sect("scalar")
            def _(e, dsub_v=dsub_v):
                e.wait_ge(self.sV.h, dsub_v)
                e.activation(decT[:], decT[:], AF.Exp)
            dec_v = self.sSC.v

            @self.sect("vector")
            def _(e, c=c):
                e.wait_ge(self.sSC.h, self.bct_done)
                e.tensor_copy(cc0[:], self.bct[64:128, c * CH:(c + 1) * CH])
            cc0_v = self.sV.v
            for qh in range(2):
                self._pe_transpose(self.bct[0:64, c * CH + qh * 128: c * CH + (qh + 1) * 128], 128,
                                   btok[:, qh * NST:(qh + 1) * NST],
                                   first_waits=[(self.sSC, self.bct_done)])
            btok_v = self.sV.v

            @self.sect("tensor")
            def _(e, c=c, cc0_v=cc0_v):
                e.wait_ge(self.sV.h, cc0_v)
                self.bank_wait_free(e, 1)
                for qh in range(2):
                    self.sPE.inc(e.matmul(self._bank(1)[:, qh * 256:(qh + 1) * 256],
                                          self.bct[0:64, c * CH + qh * 128: c * CH + (qh + 1) * 128],
                                          cc0[:], start=True, stop=True))
            g_v = self.sPE.v

            @self.sect("vector")
            def _(e, g_v=g_v):
                e.wait_ge(self.sPE.h, g_v)
                e.tensor_copy(Gs[:], self._bank(1)[:])
                self.bank_set_read(1, self.sV, self.sV.v)
            gs_v = self.sV.v

            @self.sect("vector")
            def _(e, c=c, ld_v=ld_v, gs_v=gs_v):
                e.wait_ge(self.qS.h, ld_v)
                for qh in range(2):
                    in0 = xs[:, qh * DIN:(qh + 1) * DIN].rearrange("p (h j) -> p h j", j=128)
                    in1 = self.dtT[:, (2 * c + qh) * 32:(2 * c + qh) * 32 + 32].unsqueeze(2).broadcast_to([128, 32, 128])
                    out = xsc[:, qh * DIN:(qh + 1) * DIN].rearrange("p (h j) -> p h j", j=128)
                    e.tensor_tensor(out, in0, in1, ALU.mult)
                e.tensor_tensor(Gm[:], Gs[:], self.c_mask[:], ALU.mult)
            xsc_v = self.sV.v

            bc_v = [0] * (HEADS + 2)
            d_v = [0] * (HEADS + 1)
            e_v = [0] * (HEADS + 1)
            M_v = [0] * HEADS
            y_v = [0] * HEADS
            smm_v = [0] * HEADS

            def em_sel_bc(h):
                par = h % 2
                bc_b = 1 + par
                bcb = self._bank(bc_b)

                @self.sect("vector")
                def _(e, h=h, par=par):
                    if h >= 2:
                        e.wait_ge(self.sPE.h, bc_v[h - 2])
                    e.tensor_scalar(selq[:, par * 512: par * 512 + 2 * CH],
                                    acshl[:], self.c_eye32[:, h:h + 1], None, ALU.mult)
                sel = self.sV.v

                @self.sect("tensor")
                def _(e, h=h, par=par, sel=sel, bc_b=bc_b, bcb=bcb):
                    e.wait_ge(self.sV.h, sel)
                    self.bank_wait_free(e, bc_b)
                    e.matmul(bcb[:, 0:CH], self.c_ones32[:], selq[:, par * 512: par * 512 + CH], start=True, stop=False)
                    self.sPE.inc(e.matmul(bcb[:, 0:CH], self.c_ones32[:], selq[:, par * 512 + CH: par * 512 + 2 * CH], start=False, stop=True))
                bc_v[h] = self.sPE.v

            def em_de(h):
                par = h % 2
                bcb = self._bank(1 + par)

                @self.sect("vector")
                def _(e, h=h, par=par):
                    e.wait_ge(self.sPE.h, bc_v[h])
                    if h >= 2:
                        e.wait_ge(self.sSC.h, e_v[h - 2])
                    for qh in range(2):
                        e.tensor_scalar(dq[:, par * 512 + qh * CH: par * 512 + (qh + 1) * CH], bcb[:, 0:CH],
                                        acsT[:, qh * 32 + h: qh * 32 + h + 1], 0.0, ALU.subtract, ALU.min)
                d_v[h] = self.sV.v

                @self.sect("scalar")
                def _(e, h=h, par=par):
                    e.wait_ge(self.sV.h, d_v[h])
                    if h >= 2:
                        e.wait_ge(self.sV.h, M_v[h - 2])
                    e.activation(eq[:, par * 512: par * 512 + 2 * CH], dq[:, par * 512: par * 512 + 2 * CH], AF.Exp)
                    e.activation(eacs[:, par * CH:(par + 1) * CH], bcb[0:64, 0:CH], AF.Exp)
                    self.bank_set_read(1 + par, self.sSC, self.sSC.v)
                e_v[h] = self.sSC.v

            def em_head(h):
                par = h % 2
                y_b = 3 + par
                yb = self._bank(y_b)
                s_b = 5 if par == 0 else 0

                @self.sect("vector")
                def _(e, h=h, par=par):
                    e.wait_ge(self.sSC.h, e_v[h])
                    if h >= 2:
                        e.wait_ge(self.sPE.h, y_v[h - 2])
                    e.tensor_tensor(Mq[:, par * 512: par * 512 + 2 * CH], eq[:, par * 512: par * 512 + 2 * CH],
                                    Gm[:], ALU.mult)
                    e.tensor_tensor(Ce[:, par * CH:(par + 1) * CH], eacs[:, par * CH:(par + 1) * CH], cc0[:], ALU.mult)
                M_v[h] = self.sV.v

                @self.sect("tensor")
                def _(e, h=h, par=par, y_b=y_b, yb=yb):
                    e.wait_ge(self.sV.h, M_v[h])
                    e.wait_ge(self.sV.h, sbf_ready[h])
                    self.bank_wait_free(e, y_b)
                    e.matmul(yb[:, 0:CH], xsc[:, h * HD:(h + 1) * HD], Mq[:, par * 512: par * 512 + CH], start=True, stop=False)
                    e.matmul(yb[:, 0:CH], xsc[:, DIN + h * HD: DIN + (h + 1) * HD], Mq[:, par * 512 + CH: par * 512 + 2 * CH], start=False, stop=False)
                    self.sPE.inc(e.matmul(yb[:, 0:CH], Sbf[:, h * HD:(h + 1) * HD], Ce[:, par * CH:(par + 1) * CH], start=False, stop=True))
                y_v[h] = self.sPE.v

                @self.sect("vector")
                def _(e, h=h, yb=yb, y_b=y_b):
                    e.wait_ge(self.sPE.h, y_v[h])
                    e.scalar_tensor_tensor(self.ytmp[:], xtc[:, h * CH:(h + 1) * CH],
                                           self.Dcol[:, h:h + 1], yb[:, 0:CH], ALU.mult, ALU.add)
                    e.tensor_tensor(yg[:, h * CH:(h + 1) * CH], self.ytmp[:],
                                    zsc[:, h * CH:(h + 1) * CH], ALU.mult)
                    self.bank_set_read(y_b, self.sV, self.sV.v)

                @self.sect("vector")
                def _(e, h=h, par=par, dec_v=dec_v):
                    e.wait_ge(self.sSC.h, dec_v)
                    if h >= 2:
                        e.wait_ge(self.sPE.h, smm_v[h - 2])
                    for qh in range(2):
                        e.tensor_scalar(self.bd[:, par * 2 * NST + qh * NST: par * 2 * NST + (qh + 1) * NST],
                                        btok[:, qh * NST:(qh + 1) * NST],
                                        decT[:, qh * 32 + h: qh * 32 + h + 1], None, ALU.mult)
                bd_v = self.sV.v

                @self.sect("tensor")
                def _(e, h=h, par=par, bd_v=bd_v, s_b=s_b):
                    e.wait_ge(self.sV.h, bd_v)
                    self.bank_wait_free(e, s_b)
                    e.matmul(self._bank(s_b)[0:64, 0:HD], self.bd[:, par * 2 * NST: par * 2 * NST + NST],
                             xsc[:, h * HD:(h + 1) * HD], start=True, stop=False)
                    self.sPE.inc(e.matmul(self._bank(s_b)[0:64, 0:HD], self.bd[:, par * 2 * NST + NST: par * 2 * NST + 2 * NST],
                                          xsc[:, DIN + h * HD: DIN + (h + 1) * HD], start=False, stop=True))
                smm_v[h] = self.sPE.v

                @self.sect("vector")
                def _(e, h=h, s_b=s_b, ecs_v=ecs_v):
                    e.wait_ge(self.sPE.h, smm_v[h])
                    e.wait_ge(self.sSC.h, ecs_v)
                    e.scalar_tensor_tensor(Sst[:, h * HD:(h + 1) * HD], Sst[:, h * HD:(h + 1) * HD],
                                           ecsP[0:64, h:h + 1], self._bank(s_b)[0:64, 0:HD], ALU.mult, ALU.add)
                    e.tensor_copy(Sbf[:, h * HD:(h + 1) * HD], Sst[:, h * HD:(h + 1) * HD])
                    self.bank_set_read(s_b, self.sV, self.sV.v)
                    sbf_ready[h] = self.sV.v

            em_sel_bc(0)
            em_sel_bc(1)
            em_de(0)
            for h in range(HEADS):
                if h + 2 < HEADS:
                    em_sel_bc(h + 2)
                if h + 1 < HEADS:
                    em_de(h + 1)
                em_head(h)
            prev_chunk_pe = self.sPE.v
            self.pump(2)

            ygv = self.sV.v

            @self.sect("vector")
            def _(e, ygv=ygv):
                e.wait_ge(self.sV.h, ygv)
                e.tensor_tensor(sq[:], yg[:], yg[:], ALU.mult)
            sq_v = self.sV.v

            @self.sect("tensor")
            def _(e, sq_v=sq_v):
                e.wait_ge(self.sV.h, sq_v)
                e.wait_ge(self.sV.h, self.consts_v)
                self.bank_wait_free(e, 0)
                for k in range(32):
                    inst = e.matmul(self._bank(0)[0:1, 0:CH], self.c_onesc[:],
                                    sq[:, k * CH:(k + 1) * CH], start=(k == 0), stop=(k == 31))
                self.sPE.inc(inst)
            sqmm_v = self.sPE.v

            @self.sect("scalar")
            def _(e, sqmm_v=sqmm_v):
                e.wait_ge(self.sPE.h, sqmm_v)
                e.activation(rr[0:1, 0:CH], self._bank(0)[0:1, 0:CH], AF.Ln, scale=self.c_f[0:1, 4:5], bias=self.c_f[0:1, 0:1])
                e.activation(rr[0:1, CH:2 * CH], rr[0:1, 0:CH], AF.Exp, scale=self.c_f[0:1, 1:2])
                self.bank_set_read(0, self.sSC, self.sSC.v)
            rstd_v = self.sSC.v

            @self.sect("vector")
            def _(e, rstd_v=rstd_v):
                e.wait_ge(self.sSC.h, rstd_v)
                e.tensor_copy(rstd_hi[:], rr[0:1, CH:2 * CH])
                e.tensor_tensor(rstd_lo[:], rr[0:1, CH:2 * CH], rstd_hi[:], ALU.subtract)
            rsp_v = self.sV.v

            @self.sect("tensor")
            def _(e, rsp_v=rsp_v):
                e.wait_ge(self.sV.h, rsp_v)
                self.bank_wait_free(e, 0)
                e.matmul(self._bank(0)[:, 0:CH], self.c_ones1[:], rstd_hi[:], start=True, stop=False)
                self.sPE.inc(e.matmul(self._bank(0)[:, 0:CH], self.c_ones1[:], rstd_lo[:], start=False, stop=True))
            rbc_v = self.sPE.v

            @self.sect("vector")
            def _(e, rbc_v=rbc_v, c=c, ygst=self.yg_store):
                e.wait_ge(self.sPE.h, rbc_v)
                if c >= 1:
                    e.wait_ge(self.qA.h, ygst)
                e.tensor_copy(rbc[:], self._bank(0)[:, 0:CH])
                for k in range(32):
                    e.scalar_tensor_tensor(ygb[:, k * CH:(k + 1) * CH], yg[:, k * CH:(k + 1) * CH],
                                           self.nw[:, k:k + 1], rbc[:], ALU.mult, ALU.mult)
                self.bank_set_read(0, self.sV, self.sV.v)
            ygb_v = self.sV.v

            @self.sect("scalar")
            def _(e, ygb_v=ygb_v, c=c):
                e.wait_ge(self.sV.h, ygb_v)
                dst = self.YG[:, c * CH:(c + 1) * CH].rearrange("(k p) j -> p k j", p=128)
                self.qA.inc(e.dma_start(dst, ygb.ap().rearrange("p (k j) -> p k j", k=32)), 16)
            self.yg_store = self.qA.v

        if self.dbg == "yg":
            self._dump_scratch(self.YG)

    def _w_out_pass(self):
        nc = self.nc
        with ExitStack() as pctx:
            E = pctx.enter_context
            wo = E(nc.sbuf_tensor([128, 32 * D], BF16))
            lhs = E(nc.sbuf_tensor([128, 2 * 32 * 128], BF16))
            x0t = E(nc.sbuf_tensor([128, 2 * D], F32))
            xo = E(nc.sbuf_tensor([128, 2 * D], F32))
            self.x_store = [None] * NS

            @self.sect("sync")
            def _(e):
                e.wait_ge(self.qG.h, self.cast_done["W_out"])
                e.wait_ge(self.qA.h, self.yg_store)
                src = self.W_outB.rearrange("(k p) j -> p k j", p=128)
                self.qS.inc(e.dma_start(wo.ap().rearrange("p (k j) -> p k j", k=32), src), 16)
            wo_v = self.qS.v

            lhs_used = [0, 0]
            x0_used = [0, 0]
            xo_stored = [0, 0]
            for i in range(NS):
                slot = i % 2
                lh = lhs[:, slot * 32 * 128:(slot + 1) * 32 * 128]

                @self.sect("sync")
                def _(e, i=i, slot=slot, lh=lh):
                    if i < 2:
                        e.wait_ge(self.qA.h, self.yg_store)
                    else:
                        e.wait_ge(self.sPE.h, lhs_used[slot])
                    src = self.YG[:, i * 128:(i + 1) * 128].rearrange("(k p) j -> p k j", p=128)
                    self.qS.inc(e.dma_start(lh.rearrange("p (k j) -> p k j", k=32), src), 16)
                lh_v = self.qS.v

                @self.sect("sync")
                def _(e, i=i, slot=slot):
                    if i >= 2:
                        e.wait_ge(self.sV.h, x0_used[slot])
                    else:
                        e.wait_ge(self.qA.h, self.yg_store)
                    self.qS.inc(e.dma_start(x0t[:, slot * D:(slot + 1) * D],
                                            self.u[i * 128:(i + 1) * 128, :]), 16)
                x0_v = self.qS.v

                xos = xo[:, slot * D:(slot + 1) * D]
                for nb in range(4):
                    bi = nb

                    @self.sect("tensor")
                    def _(e, lh=lh, nb=nb, bi=bi, lh_v=lh_v, wo_v=wo_v, first=(i == 0 and nb == 0)):
                        if first:
                            e.wait_ge(self.qS.h, wo_v)
                        e.wait_ge(self.qS.h, lh_v)
                        self.bank_wait_free(e, bi)
                        for k in range(32):
                            inst = e.matmul(self._bank(bi)[:], lh[:, k * 128:(k + 1) * 128],
                                            wo[:, k * D + nb * 512: k * D + (nb + 1) * 512],
                                            start=(k == 0), stop=(k == 31))
                        self.sPE.inc(inst)
                        lhs_used[slot] = self.sPE.v
                    mm_v = self.sPE.v

                    @self.sect("vector")
                    def _(e, nb=nb, bi=bi, mm_v=mm_v, x0_v=x0_v, slot=slot, xos=xos):
                        e.wait_ge(self.sPE.h, mm_v)
                        e.wait_ge(self.qS.h, x0_v)
                        if nb == 0:
                            e.wait_ge(self.qA.h, xo_stored[slot])
                        e.tensor_tensor(xos[:, nb * 512:(nb + 1) * 512],
                                                    self._bank(bi)[:], x0t[:, slot * D + nb * 512: slot * D + (nb + 1) * 512], ALU.add)
                        self.bank_set_read(bi, self.sV, self.sV.v)
                        x0_used[slot] = self.sV.v
                ev_v = self.sV.v

                @self.sect("scalar")
                def _(e, i=i, ev_v=ev_v, xos=xos):
                    e.wait_ge(self.sV.h, ev_v)
                    self.qA.inc(e.dma_start(self.X1[i * 128:(i + 1) * 128, :], xos), 16)
                xo_stored[slot] = self.qA.v
                self.x_store[i] = (self.qA, self.qA.v)
                self.pump(1)

        if self.dbg == "x1":
            @self.sect("sync")
            def _(e):
                for (ctr, v) in self.x_store:
                    e.wait_ge(ctr.h, v)
                self.qS.inc(e.dma_start(self.dbg_t, self.X1), 16)
            self.out_done = [(self.qS, self.qS.v)]

    def _gate_prep(self):
        nc = self.nc
        with ExitStack() as p:
            E = p.enter_context
            gwf = E(nc.sbuf_tensor([16, D], F32))
            gwh = E(nc.sbuf_tensor([16, D], BF16))
            gwl = E(nc.sbuf_tensor([16, D], BF16))

            @self.sect("sync")
            def _(e):
                for (ctr, v) in self.x_store:
                    e.wait_ge(ctr.h, v)
                self.qS.inc(e.dma_start(gwf[:], self.gate_w), 16)
            gl = self.qS.v

            @self.sect("vector")
            def _(e):
                e.wait_ge(self.qS.h, gl)
                e.tensor_copy(gwh[:], gwf[:])
                e.tensor_tensor(gwl[:], gwf[:], gwh[:], ALU.subtract)
            spl = self.sV.v
            for k in range(16):
                self._pe_transpose(gwh[:, k * 128:(k + 1) * 128], 128,
                                   self.gwT_hi[:, k * 16:(k + 1) * 16],
                                   first_waits=[(self.sV, spl)] if k == 0 else None)
                self._pe_transpose(gwl[:, k * 128:(k + 1) * 128], 128,
                                   self.gwT_lo[:, k * 16:(k + 1) * 16])
        self.gate_prep_done = self.sV.v

    def _ffn(self, Xin, Xout, xin_ready, call):
        nc = self.nc
        HS = S // 2
        with ExitStack() as p:
            E = p.enter_context
            self.wg = E(nc.sbuf_tensor([128, NS * 16], F32))
            t2T = E(nc.sbuf_tensor([128, NS * S], BF16))
            with ExitStack() as pn:
                self._norm_t(pn, Xin, t2T, tile_ready=xin_ready, gate=True)
            t2T_done = self.sV.v

            y = E(nc.sbuf_tensor([128, 8 * D], BF16))
            h = E(nc.sbuf_tensor([128, 11 * HS], BF16))
            wb1 = E(nc.sbuf_tensor([128, 2 * 16 * 128], BF16))
            wb3 = E(nc.sbuf_tensor([128, 2 * 16 * 128], BF16))
            w2c = E(nc.sbuf_tensor([128, 2 * 11 * 512], BF16))
            stmp = E(nc.sbuf_tensor([128, 2 * 512], F32))
            xin_t = E(nc.sbuf_tensor([128, D], F32))
            xo = E(nc.sbuf_tensor([128, D], F32))

            experts = []
            for ei in range(NEXP):
                experts.append((f"w1.{ei}", self.w1t[ei], f"w3.{ei}", self.w3t[ei],
                                f"w2.{ei}", self.w2t[ei], ei, 0, 0))
            experts.append(("sw1", self.sw1t, "sw3", self.sw3t, "sw2", self.sw2t, None, 0, 0))
            experts.append(("sw1", self.sw1t, "sw3", self.sw3t, "sw2", self.sw2t, None, 11, 11))

            self.x_store = [None] * NS
            wslot_used = [0, 0]
            w2slot_used = [0, 0]
            stmp_used = [0, 0]
            xin_used = [0, 0]
            xo_stored = [0, 0]
            for hh in range(2):
                y_first = True
                for xi, (n1, a1, n3, a3, n2, a2, ei, ib, k2b) in enumerate(experts):
                    ipairs = [(i, i + 1) for i in range(11)]
                    tpv1 = max(self.tp_done_get(n1), self.tp_done_get(n3))
                    for pi, (i0, i1) in enumerate(ipairs):
                        jw = (i1 - i0) * 128
                        slot = pi % 2
                        wb1s = wb1[:, slot * 16 * 128: slot * 16 * 128 + 16 * jw]
                        wb3s = wb3[:, slot * 16 * 128: slot * 16 * 128 + 16 * jw]

                        @self.sect("scalar")
                        def _(e, a1=a1, a3=a3, i0=i0, slot=slot, ib=ib,
                              wb1s=wb1s, wb3s=wb3s, tpv1=tpv1):
                            e.wait_ge(self.qG.h, tpv1)
                            e.wait_ge(self.sPE.h, wslot_used[slot])
                            self.qA.inc(e.dma_start(wb1s.rearrange("p (k j) -> p k j", k=16), a1[ib + i0]), 16)
                            self.qA.inc(e.dma_start(wb3s.rearrange("p (k j) -> p k j", k=16), a3[ib + i0]), 16)
                        wl_v = self.qA.v

                        for i in range(i0, i1):
                            ioff = (i - i0) * 128
                            for nb in range(2):
                                b1 = (2 * i + nb) % 2
                                b3 = 2 + (2 * i + nb) % 2

                                @self.sect("tensor")
                                def _(e, i=i, ioff=ioff, nb=nb, b1=b1, b3=b3, jw=jw, hh=hh,
                                      wb1s=wb1s, wb3s=wb3s, wl_v=wl_v,
                                      first=(xi == 0 and i == 0 and nb == 0)):
                                    if first:
                                        e.wait_ge(self.sV.h, t2T_done)
                                    e.wait_ge(self.qA.h, wl_v)
                                    self.bank_wait_free(e, b1)
                                    for k in range(16):
                                        inst = e.matmul(self._bank(b1)[:],
                                                        wb1s[:, k * jw + ioff: k * jw + ioff + 128],
                                                        t2T[:, k * S + hh * HS + nb * 512: k * S + hh * HS + (nb + 1) * 512],
                                                        start=(k == 0), stop=(k == 15))
                                    self.sPE.inc(inst)
                                    self.bank_wait_free(e, b3)
                                    for k in range(16):
                                        inst = e.matmul(self._bank(b3)[:],
                                                        wb3s[:, k * jw + ioff: k * jw + ioff + 128],
                                                        t2T[:, k * S + hh * HS + nb * 512: k * S + hh * HS + (nb + 1) * 512],
                                                        start=(k == 0), stop=(k == 15))
                                    self.sPE.inc(inst)
                                    wslot_used[slot] = self.sPE.v
                                mm_v = self.sPE.v

                                sslot = (2 * i + nb) % 2

                                @self.sect("scalar")
                                def _(e, b1=b1, mm_v=mm_v, sslot=sslot):
                                    e.wait_ge(self.sPE.h, mm_v)
                                    e.wait_ge(self.sV.h, stmp_used[sslot])
                                    e.activation(stmp[:, sslot * 512:(sslot + 1) * 512],
                                                 self._bank(b1)[:], AF.Silu)
                                    self.bank_set_read(b1, self.sSC, self.sSC.v)
                                sil_v = self.sSC.v

                                @self.sect("vector")
                                def _(e, b3=b3, sil_v=sil_v, sslot=sslot, i=i, nb=nb):
                                    e.wait_ge(self.sSC.h, sil_v)
                                    e.tensor_tensor(h[:, i * HS + nb * 512: i * HS + (nb + 1) * 512],
                                                    stmp[:, sslot * 512:(sslot + 1) * 512],
                                                    self._bank(b3)[:], ALU.mult)
                                    self.bank_set_read(b3, self.sV, self.sV.v)
                                    stmp_used[sslot] = self.sV.v
                    h_done = self.sV.v

                    tpv2 = self.tp_done_get(n2)
                    for nb in range(4):
                        slot = nb % 2
                        w2s = w2c[:, slot * 11 * 512:(slot + 1) * 11 * 512]

                        @self.sect("scalar")
                        def _(e, a2=a2, nb=nb, w2s=w2s, slot=slot, tpv2=tpv2, k2b=k2b):
                            e.wait_ge(self.qG.h, tpv2)
                            e.wait_ge(self.sPE.h, w2slot_used[slot])
                            src = a2[nb][:, k2b:k2b + 11, :]
                            self.qA.inc(e.dma_start(w2s.rearrange("p (k j) -> p k j", k=11), src), 16)
                        w2l_v = self.qA.v

                        for si in range(8):
                            b = 4 + si % 2

                            @self.sect("tensor")
                            def _(e, si=si, nb=nb, b=b, w2s=w2s, w2l_v=w2l_v, hh=hh,
                                  h_done=h_done, first=(nb == 0 and si == 0)):
                                if first:
                                    e.wait_ge(self.sV.h, h_done)
                                e.wait_ge(self.qA.h, w2l_v)
                                self.bank_wait_free(e, b)
                                for k in range(11):
                                    inst = e.matmul(self._bank(b)[:],
                                                    h[:, k * HS + si * 128: k * HS + si * 128 + 128],
                                                    w2s[:, k * 512:(k + 1) * 512],
                                                    start=(k == 0), stop=(k == 10))
                                self.sPE.inc(inst)
                                w2slot_used[slot] = self.sPE.v
                            mmv = self.sPE.v

                            @self.sect("vector")
                            def _(e, si=si, nb=nb, b=b, mmv=mmv, ei=ei, hh=hh, y_first=y_first):
                                e.wait_ge(self.sPE.h, mmv)
                                ysl = y[:, si * D + nb * 512: si * D + (nb + 1) * 512]
                                if ei is not None:
                                    wcol = self.wg[:, (hh * 8 + si) * 16 + ei: (hh * 8 + si) * 16 + ei + 1]
                                    if y_first:
                                        e.tensor_scalar(ysl, self._bank(b)[:], wcol, None, ALU.mult)
                                    else:
                                        e.scalar_tensor_tensor(ysl, self._bank(b)[:], wcol, ysl, ALU.mult, ALU.add)
                                else:
                                    e.tensor_tensor(ysl, ysl, self._bank(b)[:], ALU.add)
                                self.bank_set_read(b, self.sV, self.sV.v)
                    y_first = False
                    self.pump(2)

                for si in range(8):
                    i = hh * 8 + si

                    @self.sect("sync")
                    def _(e, i=i, si=si):
                        ctr, v = xin_ready[i]
                        e.wait_ge(ctr.h, v)
                        if xin_used[0]:
                            e.wait_ge(self.sV.h, xin_used[0])
                        self.qS.inc(e.dma_start(xin_t[:], Xin[i * 128:(i + 1) * 128, :]), 16)
                    xl_v = self.qS.v

                    @self.sect("vector")
                    def _(e, si=si, xl_v=xl_v):
                        e.wait_ge(self.qS.h, xl_v)
                        if xo_stored[0]:
                            e.wait_ge(self.qA.h, xo_stored[0])
                        e.tensor_tensor(xo[:], xin_t[:], y[:, si * D:(si + 1) * D], ALU.add)
                        xin_used[0] = self.sV.v
                    ev = self.sV.v

                    @self.sect("scalar")
                    def _(e, i=i, ev=ev):
                        e.wait_ge(self.sV.h, ev)
                        self.qA.inc(e.dma_start(Xout[i * 128:(i + 1) * 128, :], xo[:]), 16)
                    xo_stored[0] = self.qA.v
                    self.x_store[i] = (self.qA, self.qA.v)
        self.out_done = list(self.x_store)

    def _attnb(self):
        nc = self.nc
        with ExitStack() as p:
            E = p.enter_context
            rs_all = E(nc.sbuf_tensor([128, NS], F32))
            mb_all = E(nc.sbuf_tensor([128, NS], F32))
            t3T = E(nc.sbuf_tensor([128, NS * S], BF16))
            with ExitStack() as pn:
                self._norm_t(pn, self.X2, t3T, tile_ready=self.x_store,
                             save_stats=(rs_all, mb_all))
            t3T_done = self.sV.v

            stT = E(nc.sbuf_tensor([128, 8 * S], BF16))
            CTs = E(nc.sbuf_tensor([128, 8 * D], BF16))
            Db = E(nc.sbuf_tensor([128, D], F32))
            dhl = E(nc.sbuf_tensor([1, D], F32))
            dhb = E(nc.sbuf_tensor([1, D], BF16))
            dlb = E(nc.sbuf_tensor([1, D], BF16))
            btile = E(nc.sbuf_tensor([128, 2 * 16 * 128], BF16))
            x2t = E(nc.sbuf_tensor([128, 2 * D], F32))
            xo = E(nc.sbuf_tensor([128, D], F32))
            t3r = E(nc.sbuf_tensor([128, 512], F32))
            q1 = E(nc.sbuf_tensor([128, 512], F32))

            tpC = self.tp_done_get("C")
            tpB = self.tp_done_get("B")
            prior = list(self.x_store)

            @self.sect("sync")
            def _(e):
                for (ctr, v) in prior:
                    e.wait_ge(ctr.h, v)
                self.qS.inc(e.dma_start(dhl[0:1, 0:D], self.Dab), 16)
                src = self.Ct.rearrange("(k p) j -> p k j", p=128)
                e.wait_ge(self.qG.h, tpC)
                self.qS.inc(e.dma_start(CTs.ap().rearrange("p (k j) -> p k j", k=8), src), 16)
            dl_v = self.qS.v

            @self.sect("vector")
            def _(e):
                e.wait_ge(self.qS.h, dl_v)
                e.tensor_copy(dhb[:], dhl[0:1, 0:D])
                e.tensor_tensor(dlb[:], dhl[0:1, 0:D], dhb[:], ALU.subtract)
            dspl = self.sV.v

            for nb in range(4):
                b = 5

                @self.sect("tensor")
                def _(e, nb=nb, dspl=dspl):
                    if nb == 0:
                        e.wait_ge(self.sV.h, dspl)
                    self.bank_wait_free(e, 5)
                    e.matmul(self._bank(5)[:], self.c_ones1[:], dhb[0:1, nb * 512:(nb + 1) * 512], start=True, stop=False)
                    self.sPE.inc(e.matmul(self._bank(5)[:], self.c_ones1[:], dlb[0:1, nb * 512:(nb + 1) * 512], start=False, stop=True))
                dmm = self.sPE.v

                @self.sect("vector")
                def _(e, nb=nb, dmm=dmm):
                    e.wait_ge(self.sPE.h, dmm)
                    e.tensor_copy(Db[:, nb * 512:(nb + 1) * 512], self._bank(5)[:])
                    self.bank_set_read(5, self.sV, self.sV.v)

            bslot_used = [0, 0]
            for m in range(8):
                slot = m % 2
                bt = btile[:, slot * 16 * 128:(slot + 1) * 16 * 128]

                @self.sect("scalar")
                def _(e, m=m, bt=bt, slot=slot, first=(m < 2)):
                    if first:
                        e.wait_ge(self.qG.h, tpB)
                    e.wait_ge(self.sPE.h, bslot_used[slot])
                    src = self.Bt[:, m * 128:(m + 1) * 128].rearrange("(k p) j -> p k j", p=128)
                    self.qA.inc(e.dma_start(bt.rearrange("p (k j) -> p k j", k=16), src), 16)
                bl_v = self.qA.v

                for nb in range(4):
                    b = m % 2

                    @self.sect("tensor")
                    def _(e, m=m, nb=nb, b=b, bt=bt, bl_v=bl_v, slot=slot, first=(m == 0 and nb == 0)):
                        if first:
                            e.wait_ge(self.sV.h, t3T_done)
                        e.wait_ge(self.qA.h, bl_v)
                        self.bank_wait_free(e, b)
                        for k in range(16):
                            inst = e.matmul(self._bank(b)[:], bt[:, k * 128:(k + 1) * 128],
                                            t3T[:, k * S + nb * 512: k * S + (nb + 1) * 512],
                                            start=(k == 0), stop=(k == 15))
                        self.sPE.inc(inst)
                        bslot_used[slot] = self.sPE.v
                    mmv = self.sPE.v

                    @self.sect("scalar")
                    def _(e, m=m, nb=nb, b=b, mmv=mmv):
                        e.wait_ge(self.sPE.h, mmv)
                        e.activation(stT[:, m * S + nb * 512: m * S + (nb + 1) * 512],
                                     self._bank(b)[:], AF.Tanh)
                        self.bank_set_read(b, self.sSC, self.sSC.v)
            st_done = self.sSC.v

            x2_used = [0, 0]
            xo_stored = [0, 0]
            self.x_store = [None] * NS
            for si in range(NS):
                slot = si % 2

                @self.sect("sync")
                def _(e, si=si, slot=slot):
                    if si >= 2:
                        e.wait_ge(self.sV.h, x2_used[slot])
                    self.qS.inc(e.dma_start(x2t[:, slot * D:(slot + 1) * D],
                                            self.X2[si * 128:(si + 1) * 128, :]), 16)
                xl_v = self.qS.v

                for nb in range(4):
                    b = 2 + si % 2

                    @self.sect("tensor")
                    def _(e, si=si, nb=nb, b=b, st_done=st_done, dl_v=dl_v, first=(si == 0 and nb == 0)):
                        if first:
                            e.wait_ge(self.sSC.h, st_done)
                            e.wait_ge(self.qS.h, dl_v)
                        self.bank_wait_free(e, b)
                        for k in range(8):
                            inst = e.matmul(self._bank(b)[:],
                                            stT[:, k * S + si * 128: k * S + si * 128 + 128],
                                            CTs[:, k * D + nb * 512: k * D + (nb + 1) * 512],
                                            start=(k == 0), stop=(k == 7))
                        self.sPE.inc(inst)
                    mmv = self.sPE.v

                    @self.sect("vector")
                    def _(e, si=si, nb=nb, b=b, mmv=mmv, xl_v=xl_v, slot=slot):
                        e.wait_ge(self.sPE.h, mmv)
                        e.wait_ge(self.qS.h, xl_v)
                        if nb == 0 and xo_stored[0]:
                            e.wait_ge(self.qA.h, xo_stored[0])
                        xsl = x2t[:, slot * D + nb * 512: slot * D + (nb + 1) * 512]
                        e.tensor_scalar(t3r[:], xsl, rs_all[:, si:si + 1], mb_all[:, si:si + 1], ALU.mult, ALU.add)
                        e.tensor_tensor(q1[:], t3r[:], Db[:, nb * 512:(nb + 1) * 512], ALU.mult)
                        e.tensor_tensor(q1[:], q1[:], self._bank(b)[:], ALU.add)
                        e.tensor_tensor(xo[:, nb * 512:(nb + 1) * 512], q1[:], xsl, ALU.add)
                        self.bank_set_read(b, self.sV, self.sV.v)
                        x2_used[slot] = self.sV.v
                ev = self.sV.v

                @self.sect("scalar")
                def _(e, si=si, ev=ev):
                    e.wait_ge(self.sV.h, ev)
                    self.qA.inc(e.dma_start(self.X3[si * 128:(si + 1) * 128, :], xo[:]), 16)
                xo_stored[0] = self.qA.v
                self.x_store[si] = (self.qA, self.qA.v)

        if self.dbg == "x3":
            @self.sect("sync")
            def _(e):
                for (ctr, v) in self.x_store:
                    e.wait_ge(ctr.h, v)
                self.qS.inc(e.dma_start(self.dbg_t, self.X3), 16)
            self.out_done = [(self.qS, self.qS.v)]

    def tp_done_get(self, name):
        while self.prepass_left and name not in self.tp_done:
            next(self.prepass_gen)
        return self.tp_done[name][1]

    def _finish(self):
        if not hasattr(self, "out_done"):
            self.out_done = [(self.qA, self.qA.v)]

        @self.sect("sync")
        def _(e):
            for (ctr, v) in self.out_done:
                e.wait_ge(ctr.h, v)


def build(nc, dbg=DEBUG_OUT):
    _Builder(nc, dbg).build()
    return nc


def _mk_consts():
    idb = np.eye(128, dtype=np.float32).astype(ml_dtypes.bfloat16)
    ones32 = np.ones((32, 128), dtype=np.float32).astype(ml_dtypes.bfloat16)
    ones1 = np.ones((1, 128), dtype=np.float32).astype(ml_dtypes.bfloat16)
    eye32 = np.eye(32, dtype=np.float32)
    m = np.zeros((128, 512), dtype=np.float32)
    for qh in range(2):
        s_idx = np.arange(128)[:, None] + qh * 128
        l_idx = np.arange(256)[None, :]
        m[:, qh * 256:(qh + 1) * 256] = (s_idx <= l_idx).astype(np.float32)
    return idb, ones32, ones1, eye32, m.astype(ml_dtypes.bfloat16)


def make_inputs_for_core(b, full):
    idb, ones32, ones1, eye32, mask = _mk_consts()
    d = {
        "u": full["L0_attn_a_u"][b],
        "W_in": full["L0_attn_a_W_in"],
        "conv_w": full["L0_attn_a_conv_w"].reshape(33, 128, 4).transpose(1, 0, 2).reshape(128, 132),
        "conv_b": full["L0_attn_a_conv_b"].reshape(33, 128).T,
        "dt_bias": full["L0_attn_a_dt_bias"].reshape(32, 1),
        "A_log": full["L0_attn_a_A_log"].reshape(32, 1),
        "Dh": full["L0_attn_a_D"].reshape(32, 1),
        "norm_w": full["L0_attn_a_norm_w"].reshape(32, 128).T,
        "W_out": full["L0_attn_a_W_out"],
        "Bw": full["L0_attn_b_B"],
        "Cw": full["L0_attn_b_C"],
        "Dab": full["L0_attn_b_D"].reshape(1, 2048),
        "gate_w": full["L0_ffn_gate_w"],
        "w1": full["L0_ffn_w1"],
        "w2": full["L0_ffn_w2"],
        "w3": full["L0_ffn_w3"],
        "sw1": full["L0_ffn_sw1"],
        "sw2": full["L0_ffn_sw2"],
        "sw3": full["L0_ffn_sw3"],
        "id_bf": idb, "ones32_bf": ones32, "ones1_bf": ones1,
        "eye32_f": eye32, "mask_bf": mask,
        "f_const": np.tile(np.array([1e-5, -0.5, -1.0, 1.0, 1.0 / 4096, 0.0, 0.0, 0.0], np.float32), (128, 1)),
    }
    return {k: np.ascontiguousarray(v) for k, v in d.items()}


def kernel(**inputs):
    nc = bass.Bass("TRN2", target_bir_lowering=False)
    build(nc)
    in_maps = [make_inputs_for_core(b, inputs) for b in range(8)]
    res = run_bass_kernel_spmd(nc, in_maps, core_ids=list(range(8)))
    return np.stack([r["out"] for r in res.results], axis=0)


# version 04
# speedup vs baseline: 1.0327x; 1.0327x over previous
import numpy as np
import ml_dtypes
from contextlib import ExitStack

import concourse.bass as bass
import concourse.mybir as mybir
from concourse.bass_utils import run_bass_kernel_spmd

F32 = mybir.dt.float32
BF16 = mybir.dt.bfloat16
AF = mybir.ActivationFunctionType
ALU = mybir.AluOpType
AX = mybir.AxisListType

S = 2048
D = 2048
NS = 16
DIN = 4096
HEADS = 32
HD = 128
NST = 64
DPROJ = 8352
CH = 256
NCH = 8
INTER = 1408
NEXP = 16
SINTER = 2816

DEBUG_OUT = None


class Ctr:
    def __init__(self, ctx, nc, name):
        self.h = ctx.enter_context(nc.semaphore(name))
        self.v = 0

    def inc(self, inst, n=1):
        inst.then_inc(self.h, n)
        self.v += n
        return self.v


_AUTO_OPS = {
    "activation", "add", "mul", "copy", "sign", "sqrt", "square", "memset",
    "memzero", "tensor_copy", "tensor_tensor", "tensor_scalar",
    "scalar_tensor_tensor", "tensor_reduce", "reduce_max", "reduce_sum",
    "bn_stats", "bn_aggr", "reciprocal", "tensor_tensor_scan", "select",
    "tensor_add", "tensor_mul", "tensor_sub", "tensor_max",
}


class SafeEng:

    def __init__(self, e, ctr):
        self._e = e
        self._c = ctr

    def __getattr__(self, name):
        f = getattr(self._e, name)
        if name not in _AUTO_OPS:
            return f

        def g(*a, **k):
            self._e.wait_ge(self._c.h, self._c.v)
            inst = f(*a, **k)
            self._c.inc(inst)
            return inst
        return g


class _Builder:
    def __init__(self, nc, dbg):
        self.nc = nc
        self.dbg = dbg

    def sect(self, eng):
        dec = getattr(self.block, eng)
        ctr = {"vector": None, "scalar": None}
        if eng == "vector":
            c = self.sV
        elif eng == "scalar":
            c = self.sSC
        else:
            c = None
        if c is None:
            return dec

        def mydec(f):
            def g(e):
                return f(SafeEng(e, c))
            return dec(g)
        return mydec

    def bank_wait_free(self, e, bi):
        for (ctr, v) in self.bk_rd[bi]:
            e.wait_ge(ctr.h, v)
        self.bk_rd[bi] = []

    def bank_set_read(self, bi, ctr, v):
        self.bk_rd[bi].append((ctr, v))

    def build(self):
        nc = self.nc
        dbg = self.dbg
        di = lambda n, sh: nc.dram_tensor(n, sh, F32, kind="ExternalInput").ap()
        self.u = di("u", [S, D])
        self.W_in = di("W_in", [D, DPROJ])
        self.conv_w = di("conv_w", [128, 33 * 4])
        self.conv_b = di("conv_b", [128, 33])
        self.dt_bias = di("dt_bias", [32, 1])
        self.A_log = di("A_log", [32, 1])
        self.Dh = di("Dh", [32, 1])
        self.norm_w = di("norm_w", [128, 32])
        self.W_out = di("W_out", [DIN, D])
        self.Bw = di("Bw", [1024, D])
        self.Cw = di("Cw", [D, 1024])
        self.Dab = di("Dab", [1, D])
        self.gate_w = di("gate_w", [NEXP, D])
        self.w1 = di("w1", [NEXP, INTER, D])
        self.w2 = di("w2", [NEXP, D, INTER])
        self.w3 = di("w3", [NEXP, INTER, D])
        self.sw1 = di("sw1", [SINTER, D])
        self.sw2 = di("sw2", [D, SINTER])
        self.sw3 = di("sw3", [SINTER, D])
        self.id_bf = nc.dram_tensor("id_bf", [128, 128], BF16, kind="ExternalInput").ap()
        self.ones32_bf = nc.dram_tensor("ones32_bf", [32, 128], BF16, kind="ExternalInput").ap()
        self.ones1_bf = nc.dram_tensor("ones1_bf", [1, 128], BF16, kind="ExternalInput").ap()
        self.eye32_f = nc.dram_tensor("eye32_f", [32, 32], F32, kind="ExternalInput").ap()
        self.mask_bf = nc.dram_tensor("mask_bf", [128, 512], BF16, kind="ExternalInput").ap()
        self.f_const = nc.dram_tensor("f_const", [128, 8], F32, kind="ExternalInput").ap()

        self.out = nc.dram_tensor("out", [S, D], F32, kind="ExternalOutput").ap()
        dit = lambda n, sh, dt: nc.dram_tensor(n, sh, dt, kind="Internal").ap()
        self.X1 = dit("X1", [S, D], F32)
        self.X2 = dit("X2", [S, D], F32)
        self.X3 = dit("X3", [S, D], F32)
        self.W_inB = dit("W_inB", [D, DPROJ], BF16)
        self.W_outB = dit("W_outB", [DIN, D], BF16)
        self.XT = dit("XTs", [DIN, S], BF16)
        self.ZS = dit("ZSs", [DIN, S], BF16)
        self.YG = dit("YGs", [DIN, S], BF16)
        self.w1n = dit("w1n", [NEXP, INTER, D], BF16)
        self.w2n = dit("w2n", [NEXP, D, INTER], BF16)
        self.w3n = dit("w3n", [NEXP, INTER, D], BF16)
        self.sw1n = dit("sw1n", [SINTER, D], BF16)
        self.sw2n = dit("sw2n", [D, SINTER], BF16)
        self.sw3n = dit("sw3n", [SINTER, D], BF16)
        self.Bn = dit("Bn", [1024, D], BF16)
        self.Cn = dit("Cn", [D, 1024], BF16)
        self.w1t = dit("w1t", [NEXP, 11, 128, 16, 128], BF16)
        self.w2t = dit("w2t", [NEXP, 4, 128, 11, 512], BF16)
        self.w3t = dit("w3t", [NEXP, 11, 128, 16, 128], BF16)
        self.sw1t = dit("sw1t", [22, 128, 16, 128], BF16)
        self.sw2t = dit("sw2t", [4, 128, 22, 512], BF16)
        self.sw3t = dit("sw3t", [22, 128, 16, 128], BF16)
        self.Bt = dit("Bt", [D, 1024], BF16)
        self.Ct = dit("Ct", [1024, D], BF16)

        if dbg:
            shp = {"t1T": [128, NS * S], "zs": [DIN, S], "xt": [DIN, S], "dtv": [32, S],
                   "x1": [S, D], "yg": [DIN, S], "x2": [S, D], "x3": [S, D]}[dbg]
            dt_ = BF16 if dbg in ("t1T", "zs", "xt", "yg") else F32
            self.dbg_t = nc.dram_tensor("dbg", shp, dt_, kind="ExternalOutput").ap()

        with ExitStack() as ctx:
            E = ctx.enter_context
            self.ctx = ctx
            self.c_id = E(nc.sbuf_tensor([128, 128], BF16))
            self.c_ones32 = E(nc.sbuf_tensor([32, 128], BF16))
            self.c_ones1 = E(nc.sbuf_tensor([1, 128], BF16))
            self.c_onesc = E(nc.sbuf_tensor([128, 1], BF16))
            self.c_eye32 = E(nc.sbuf_tensor([32, 32], F32))
            self.c_mask = E(nc.sbuf_tensor([128, 512], BF16))
            self.c_f = E(nc.sbuf_tensor([128, 8], F32))
            self.ones_a = E(nc.sbuf_tensor([32, CH], F32))
            self.csel_hi = E(nc.sbuf_tensor([32, 32], BF16))
            self.csel_lo = E(nc.sbuf_tensor([32, 32], BF16))
            self.eyeD = E(nc.sbuf_tensor([32, 32], F32))
            self.eyeD_bf = E(nc.sbuf_tensor([32, 32], BF16))
            self.gate_tmp = E(nc.sbuf_tensor([128, 16], F32))
            self.gate_tmp2 = E(nc.sbuf_tensor([128, 16], F32))
            self.gate_col = E(nc.sbuf_tensor([128, 4], F32))
            self.gwT_hi = E(nc.sbuf_tensor([128, 16 * 16], BF16))
            self.gwT_lo = E(nc.sbuf_tensor([128, 16 * 16], BF16))
            self.ytmp = E(nc.sbuf_tensor([128, CH], F32))
            self.bd = E(nc.sbuf_tensor([128, 4 * NST], BF16))
            self.krow = E(nc.sbuf_tensor([128, 2 * SINTER], BF16))

            self.qS = Ctr(ctx, nc, "qS")
            self.qA = Ctr(ctx, nc, "qA")
            self.qG = Ctr(ctx, nc, "qG")
            self.sPE = Ctr(ctx, nc, "sPE")
            self.sV = Ctr(ctx, nc, "sV")
            self.sSC = Ctr(ctx, nc, "sSC")
            self.block = E(nc.Block())
            self.bkt = [E(nc.psum_tensor(f"bank{i}", [128, 512], F32)) for i in range(6)]
            self.bk_rd = [[] for _ in range(6)]
            self.psT = [E(nc.psum_tensor(f"psT{i}", [128, 256], BF16)) for i in range(2)]
            self.psT_rd = [[], []]

            self._load_consts()
            self._emit_prepass_casts()
            self.prepass_gen = self._prepass_transpose_gen()
            self.prepass_left = True

            self._mamba()
            if dbg in ("t1T", "zs", "xt", "dtv", "yg", "x1"):
                self._finish()
                return
            self._gate_prep()
            self._ffn(self.X1, self.X2, self.x_store, 0)
            if dbg == "x2":
                @self.sect("sync")
                def _(e):
                    for (ctr, v) in self.x_store:
                        e.wait_ge(ctr.h, v)
                    self.qS.inc(e.dma_start(self.dbg_t, self.X2), 16)
                self.out_done = [(self.qS, self.qS.v)]
                self._finish()
                return
            self._attnb()
            if dbg == "x3":
                self._finish()
                return
            self._ffn(self.X3, self.out, self.x_store, 1)
            self._finish()

    def _bank(self, i):
        return self.bkt[i]

    def _load_consts(self):
        q = self.qS

        @self.sect("sync")
        def _(e):
            q.inc(e.dma_start(self.c_id[:], self.id_bf), 16)
            q.inc(e.dma_start(self.c_ones32[:], self.ones32_bf), 16)
            q.inc(e.dma_start(self.c_ones1[:], self.ones1_bf), 16)
            q.inc(e.dma_start(self.c_eye32[:], self.eye32_f), 16)
            q.inc(e.dma_start(self.c_mask[:], self.mask_bf), 16)
            q.inc(e.dma_start(self.c_f[:], self.f_const), 16)
        self.consts_done = q.v

        @self.sect("vector")
        def _(e):
            e.memset(self.ones_a[:], 1.0)
            e.memset(self.c_onesc[:], 1.0)
        self.consts_v = self.sV.v

    def _emit_prepass_casts(self):
        qG = self.qG
        self.cast_done = {}

        @self.sect("gpsimd")
        def _(e):
            fbl = [(f, 512) for f in range(0, 8192, 512)] + [(8192, 160)]
            for fbi, (f0, fbw) in enumerate(fbl):
                qG.inc(e.dma_start(self.W_inB[:, f0:f0 + fbw], self.W_in[:, f0:f0 + fbw]), 16)
                self.cast_done[f"W_in.{fbi}"] = qG.v
            self.cast_done["W_in"] = qG.v
            for ei in range(NEXP):
                qG.inc(e.dma_start(self.w1n[ei], self.w1[ei]), 16)
                self.cast_done[f"w1.{ei}"] = qG.v
                qG.inc(e.dma_start(self.w3n[ei], self.w3[ei]), 16)
                self.cast_done[f"w3.{ei}"] = qG.v
                qG.inc(e.dma_start(self.w2n[ei], self.w2[ei]), 16)
                self.cast_done[f"w2.{ei}"] = qG.v
                if ei == 1:
                    qG.inc(e.dma_start(self.W_outB, self.W_out), 16)
                    self.cast_done["W_out"] = qG.v
            qG.inc(e.dma_start(self.sw1n, self.sw1), 16)
            self.cast_done["sw1"] = qG.v
            qG.inc(e.dma_start(self.sw3n, self.sw3), 16)
            self.cast_done["sw3"] = qG.v
            qG.inc(e.dma_start(self.sw2n, self.sw2), 16)
            self.cast_done["sw2"] = qG.v
            qG.inc(e.dma_start(self.Bn, self.Bw), 16)
            self.cast_done["B"] = qG.v
            qG.inc(e.dma_start(self.Cn, self.Cw), 16)
            self.cast_done["C"] = qG.v

    def _prepass_transpose_gen(self):
        jobs = []
        for ei in range(NEXP):
            jobs.append((f"w1.{ei}", self.w1n[ei], self.w1t[ei], INTER, D, "i128"))
            jobs.append((f"w3.{ei}", self.w3n[ei], self.w3t[ei], INTER, D, "i128"))
            jobs.append((f"w2.{ei}", self.w2n[ei], self.w2t[ei], D, INTER, "nb512"))
        jobs.append(("sw1", self.sw1n, self.sw1t, SINTER, D, "i128"))
        jobs.append(("sw3", self.sw3n, self.sw3t, SINTER, D, "i128"))
        jobs.append(("sw2", self.sw2n, self.sw2t, D, SINTER, "nb512"))
        jobs.append(("B", self.Bn, self.Bt, 1024, D, "plain"))
        jobs.append(("C", self.Cn, self.Ct, D, 1024, "plain"))
        self.tp_done = {}
        slot = 0
        slot_store_v = [0, 0]
        qS, qG = self.qS, self.qG
        for name, nat, tp, O, I, kind in jobs:
            castv = self.cast_done[name]
            for k in range(I // 128):
                kb = self.krow[:, slot * SINTER: slot * SINTER + O]
                sv = slot_store_v[slot]

                @self.sect("sync")
                def _(e, kb=kb, nat=nat, k=k, castv=castv, sv=sv):
                    e.wait_ge(qG.h, max(castv, sv))
                    qS.inc(e.dma_start(kb, nat[:, k * 128:(k + 1) * 128], transpose=True), 16)
                loadv = qS.v

                @self.sect("gpsimd")
                def _(e, kb=kb, tp=tp, k=k, loadv=loadv, kind=kind):
                    e.wait_ge(qS.h, loadv)
                    if kind == "i128":
                        dst = tp[:, :, k, :].rearrange("i p j -> p i j")
                        src = kb.rearrange("p (i j) -> p i j", j=128)
                    elif kind == "nb512":
                        dst = tp[:, :, k, :].rearrange("nb p j -> p nb j")
                        src = kb.rearrange("p (nb j) -> p nb j", j=512)
                    else:
                        dst = tp[k * 128:(k + 1) * 128, :]
                        src = kb
                    qG.inc(e.dma_start(dst, src), 16)
                slot_store_v[slot] = qG.v
                slot = (slot + 1) % 2
                yield
            self.tp_done[name] = (qG, qG.v)
        self.prepass_left = False
        while True:
            yield

    def pump(self, n):
        if not self.prepass_left:
            return
        for _ in range(n):
            next(self.prepass_gen)
            if not self.prepass_left:
                break

    def wait_tp(self, e, name):
        while self.prepass_left and name not in self.tp_done:
            next(self.prepass_gen)
        ctr, v = self.tp_done[name]
        e.wait_ge(ctr.h, v)

    def _pe_transpose(self, src_ap, w, dst_ap, dst_is_add_pair=False, first_waits=None):
        pslot = self.psT_slot
        self.psT_slot ^= 1
        ps = self.psT[pslot]
        npart = (src_ap[0] if dst_is_add_pair else src_ap).shape[0]
        ident = self.c_id[0:npart, 0:npart]

        @self.sect("tensor")
        def _(e):
            if first_waits:
                for (ctr, v) in first_waits:
                    e.wait_ge(ctr.h, v)
            for (ctr, v) in self.psT_rd[pslot]:
                e.wait_ge(ctr.h, v)
            self.psT_rd[pslot] = []
            if dst_is_add_pair:
                e.transpose(ps[0:w, 0:npart], src_ap[0], ident)
                self.sPE.inc(e.transpose(ps[0:w, npart:2 * npart], src_ap[1], ident))
            else:
                self.sPE.inc(e.transpose(ps[0:w, 0:npart], src_ap, ident))
        pev = self.sPE.v

        @self.sect("vector")
        def _(e):
            e.wait_ge(self.sPE.h, pev)
            if dst_is_add_pair:
                e.tensor_copy(dst_ap, ps[0:w, 0:npart])
                e.tensor_tensor(dst_ap, dst_ap, ps[0:w, npart:2 * npart], ALU.add)
            else:
                e.tensor_copy(dst_ap, ps[0:w, 0:npart])
            self.psT_rd[pslot].append((self.sV, self.sV.v))
        return self.sV.v

    _norm_seq = [0]

    def _norm_t(self, pctx, X, tT, tile_ready=None, gate=False, save_stats=None):
        nc = self.nc
        E = pctx.enter_context
        self._norm_seq[0] += 1
        nsq = self._norm_seq[0]
        xa = E(nc.sbuf_tensor([128, 2 * D], F32))
        bn = E(nc.sbuf_tensor([128, 24], F32))
        st = E(nc.sbuf_tensor([128, 8], F32))
        tf = E(nc.sbuf_tensor(f"nt_tf{nsq}", [128, D], F32)) if gate else None
        tb = E(nc.sbuf_tensor([128, D], BF16))
        tlo = E(nc.sbuf_tensor(f"nt_tlo{nsq}", [128, D], BF16)) if gate else None
        tloT = E(nc.sbuf_tensor(f"nt_tloT{nsq}", [128, 16 * 128], BF16)) if gate else None
        act_done = [0] * NS
        gate_done = []

        for i in range(NS):
            slot = i % 2
            xt = xa[:, slot * D:(slot + 1) * D]

            @self.sect("sync")
            def _(e, i=i, xt=xt, slot=slot):
                if i >= 2:
                    e.wait_ge(self.sV.h, act_done[i - 2])
                if tile_ready is not None:
                    if i == 0:
                        for (ctr, v) in tile_ready:
                            e.wait_ge(ctr.h, v)
                    else:
                        ctr, v = tile_ready[i]
                        e.wait_ge(ctr.h, v)
                self.qS.inc(e.dma_start(xt, X[i * 128:(i + 1) * 128, :]), 16)
            loadv = self.qS.v

            @self.sect("vector")
            def _(e, xt=xt, loadv=loadv):
                e.wait_ge(self.qS.h, loadv)
                for j in range(4):
                    e.bn_stats(bn[:, j * 6:(j + 1) * 6], xt[:, j * 512:(j + 1) * 512])
                e.bn_aggr(st[:, 0:2], bn[:])
            aggv = self.sV.v

            @self.sect("scalar")
            def _(e, aggv=aggv):
                e.wait_ge(self.sV.h, aggv)
                e.activation(st[:, 4:5], st[:, 1:2], AF.Ln, bias=self.c_f[:, 0:1])
                e.activation(st[:, 2:3], st[:, 4:5], AF.Exp, scale=self.c_f[:, 1:2])
            rstdv = self.sSC.v

            @self.sect("vector")
            def _(e, rstdv=rstdv):
                e.wait_ge(self.sSC.h, rstdv)
                e.tensor_scalar(st[:, 3:4], st[:, 0:1], st[:, 2:3], -1.0, ALU.mult, ALU.mult)
            mbv = self.sV.v

            if save_stats is not None:
                @self.sect("scalar")
                def _(e, mbv=mbv, i=i):
                    e.wait_ge(self.sV.h, mbv)
                    rs_all, mb_all = save_stats
                    e.copy(rs_all[:, i:i + 1], st[:, 2:3])
                    e.copy(mb_all[:, i:i + 1], st[:, 3:4])

            @self.sect("vector")
            def _(e, xt=xt, i=i):
                if gate:
                    e.tensor_scalar(tf[:], xt, st[:, 2:3], st[:, 3:4], ALU.mult, ALU.add)
                    e.tensor_copy(tb[:], tf[:])
                    e.tensor_tensor(tlo[:], tf[:], tb[:], ALU.subtract)
                else:
                    e.tensor_scalar(tb[:], xt, st[:, 2:3], st[:, 3:4], ALU.mult, ALU.add)
            actv = self.sV.v
            act_done[i] = actv
            tlov = actv

            fw = [(self.sV, actv), (self.qS, self.consts_done)]
            for k in range(16):
                self._pe_transpose(tb[:, k * 128:(k + 1) * 128], 128,
                                   tT[:, k * S + i * 128: k * S + (i + 1) * 128],
                                   first_waits=fw if k == 0 else None)
                if gate:
                    self._pe_transpose(tlo[:, k * 128:(k + 1) * 128], 128,
                                       tloT[:, k * 128:(k + 1) * 128],
                                       first_waits=[(self.sV, tlov)] if k == 0 else None)
            tpv = self.sV.v

            if gate:
                bi = 0
                bank = self._bank(bi)

                @self.sect("tensor")
                def _(e, i=i, tpv=tpv, bank=bank, bi=bi):
                    e.wait_ge(self.sV.h, tpv)
                    self.bank_wait_free(e, bi)
                    nmm = 48
                    j = 0
                    for k in range(16):
                        tTk = tT[:, k * S + i * 128: k * S + (i + 1) * 128]
                        tlk = tloT[:, k * 128:(k + 1) * 128]
                        for (l, r) in ((tTk, self.gwT_hi[:, k * 16:(k + 1) * 16]),
                                       (tTk, self.gwT_lo[:, k * 16:(k + 1) * 16]),
                                       (tlk, self.gwT_hi[:, k * 16:(k + 1) * 16])):
                            inst = e.matmul(bank[:, 0:16], l, r, start=(j == 0), stop=(j == nmm - 1))
                            j += 1
                    self.sPE.inc(inst)
                gmv = self.sPE.v
                self._gate_softmax(i, bi, gmv)
                gate_done.append(self.sV.v)
        return act_done

    def _gate_softmax(self, i, bi, gmv):
        g = self.gate_tmp
        g2 = self.gate_tmp2
        col = self.gate_col

        @self.sect("vector")
        def _(e):
            e.wait_ge(self.sPE.h, gmv)
            e.tensor_copy(g[:], self._bank(bi)[:, 0:16])
            e.tensor_reduce(col[:, 0:1], g[:], AX.X, ALU.max)
            self.bank_set_read(bi, self.sV, self.sV.v)
        v1 = self.sV.v

        @self.sect("scalar")
        def _(e):
            e.wait_ge(self.sV.h, v1)
            e.mul(col[:, 1:2], col[:, 0:1], -1.0)
        negv = self.sSC.v

        @self.sect("scalar")
        def _(e):
            e.wait_ge(self.sSC.h, negv)
            e.activation(g[:], g[:], AF.Exp, bias=col[:, 1:2])
        ev = self.sSC.v

        @self.sect("vector")
        def _(e, i=i):
            e.wait_ge(self.sSC.h, ev)
            e.tensor_reduce(col[:, 2:3], g[:], AX.X, ALU.add)
            e.reciprocal(col[:, 2:3], col[:, 2:3])
            e.tensor_scalar(g[:], g[:], col[:, 2:3], None, ALU.mult)
            e.tensor_reduce(col[:, 0:1], g[:], AX.X, ALU.max)
            e.tensor_scalar(g2[:], g[:], col[:, 0:1], None, ALU.is_lt)
            e.tensor_tensor(g2[:], g2[:], g[:], ALU.mult)
            e.tensor_reduce(col[:, 3:4], g2[:], AX.X, ALU.max)
            e.tensor_scalar(g2[:], g[:], col[:, 3:4], None, ALU.is_ge)
            e.tensor_tensor(self.wg[:, i * 16:(i + 1) * 16], g2[:], g[:], ALU.mult)

    def _mamba(self):
        nc = self.nc
        self.psT_slot = 0
        with ExitStack() as pers:
            PE = pers.enter_context
            self.bct = PE(nc.sbuf_tensor([128, S], BF16))
            self.A_dt = PE(nc.sbuf_tensor([32, S], F32))
            self.dtT = PE(nc.sbuf_tensor([128, NS * 32], F32))
            self.cDh = PE(nc.sbuf_tensor([32, 1], F32))
            self.Dcol = PE(nc.sbuf_tensor([128, 32], F32))
            self.nw = PE(nc.sbuf_tensor([128, 32], F32))
            with ExitStack() as pctx:
                E = pctx.enter_context
                t1T = E(nc.sbuf_tensor([128, NS * S], BF16))
                act_done = self._norm_t(pctx, self.u, t1T)
                t1T_done = self.sV.v

                if self.dbg == "t1T":
                    @self.sect("sync")
                    def _(e):
                        e.wait_ge(self.sV.h, t1T_done)
                        self.qS.inc(e.dma_start(self.dbg_t, t1T[:]), 16)
                    self.out_done = [(self.qS, self.qS.v)]
                    return

                self._w_in_pass(pctx, t1T, t1T_done)
                if self.dbg in ("zs", "xt", "dtv"):
                    return
            with ExitStack() as p2:
                self._ssd(p2)
                if self.dbg == "yg":
                    return
        self._w_out_pass()

    def _w_in_pass(self, pctx, t1T, t1T_done):
        nc = self.nc
        E = pctx.enter_context
        wf = E(nc.sbuf_tensor([128, 2 * 16 * 512], BF16))
        xpad = E(nc.sbuf_tensor([128, 3 + S], F32))
        acc0 = E(nc.sbuf_tensor([128, S], F32))
        acc1 = E(nc.sbuf_tensor([128, S], F32))
        row = E(nc.sbuf_tensor([128, S], BF16))
        cw = E(nc.sbuf_tensor([128, 33 * 4], F32))
        cb = E(nc.sbuf_tensor([128, 33], F32))
        dtraw = E(nc.sbuf_tensor([32, S], F32))
        dthb = E(nc.sbuf_tensor([32, S], BF16))
        dtlb = E(nc.sbuf_tensor([32, S], BF16))
        smallv = E(nc.sbuf_tensor([32, 4], F32))
        self.dt_sp = E(nc.sbuf_tensor([32, S], F32))

        @self.sect("sync")
        def _(e):
            self.qS.inc(e.dma_start(cw[:], self.conv_w), 16)
            self.qS.inc(e.dma_start(cb[:], self.conv_b), 16)
            self.qS.inc(e.dma_start(smallv[:, 0:1], self.dt_bias), 16)
            self.qS.inc(e.dma_start(smallv[:, 1:2], self.A_log), 16)
            self.qS.inc(e.dma_start(self.cDh[:], self.Dh), 16)
            self.qS.inc(e.dma_start(self.nw[:], self.norm_w), 16)
        sm_load = self.qS.v

        @self.sect("scalar")
        def _(e):
            e.wait_ge(self.qS.h, sm_load)
            e.activation(smallv[:, 2:3], smallv[:, 1:2], AF.Exp)
            e.mul(smallv[:, 2:3], smallv[:, 2:3], -1.0)
        expA_done = self.sSC.v

        @self.sect("vector")
        def _(e):
            e.wait_ge(self.qS.h, sm_load)
            e.wait_ge(self.sV.h, self.consts_v)
            e.tensor_scalar(self.eyeD[:], self.c_eye32[:], self.cDh[:], None, ALU.mult)
            e.tensor_copy(self.eyeD_bf[:], self.eyeD[:])
            e.tensor_tensor(self.eyeD[:], self.eyeD[:], self.eyeD_bf[:], ALU.subtract)
            e.tensor_copy(self.csel_lo[:], self.eyeD[:])
        dselv = self.sV.v
        bi = 0

        @self.sect("tensor")
        def _(e):
            e.wait_ge(self.sV.h, dselv)
            e.wait_ge(self.qS.h, self.consts_done)
            self.bank_wait_free(e, bi)
            e.matmul(self._bank(bi)[:, 0:32], self.c_ones32[:], self.eyeD_bf[:], start=True, stop=False)
            self.sPE.inc(e.matmul(self._bank(bi)[:, 0:32], self.c_ones32[:], self.csel_lo[:], start=False, stop=True))
        dmmv = self.sPE.v

        @self.sect("vector")
        def _(e):
            e.wait_ge(self.sPE.h, dmmv)
            e.tensor_copy(self.Dcol[:], self._bank(bi)[:, 0:32])
            self.bank_set_read(bi, self.sV, self.sV.v)
        self.Dcol_done = self.sV.v

        fblocks = [(f, 512) for f in range(0, 8192, 512)] + [(8192, 160)]
        mm_done_for_slot = [0, 0]
        row_store_q = 0
        self.dtraw_done = 0
        self.bct_done = 0

        def emit_load(fbi):
            f0, fbw = fblocks[fbi]
            slot = fbi % 2
            wfs = wf[:, slot * 16 * 512: slot * 16 * 512 + 16 * fbw]

            @self.sect("scalar")
            def _(e, f0=f0, fbw=fbw, wfs=wfs, slot=slot, fbi=fbi):
                e.wait_ge(self.qG.h, self.cast_done[f"W_in.{fbi}"])
                if fbi >= 2:
                    e.wait_ge(self.sPE.h, mm_done_for_slot[slot])
                src = self.W_inB[:, f0:f0 + fbw].rearrange("(k p) j -> p k j", p=128)
                self.qA.inc(e.dma_start(wfs.rearrange("p (k j) -> p k j", k=16), src), 16)
            return wfs, self.qA.v

        loads = {0: emit_load(0), 1: emit_load(1)}
        for fbi, (f0, fbw) in enumerate(fblocks):
            wfs, wf_load = loads.pop(fbi)
            slot = fbi % 2
            nm = (fbw + 127) // 128
            for m in range(nm):
                mw = min(128, fbw - m * 128)
                ftile = f0 + m * 128
                for nb in range(4):
                    bi2 = nb
                    bank = self._bank(bi2)
                    first = (m == 0 and nb == 0)

                    @self.sect("tensor")
                    def _(e, wfs=wfs, m=m, nb=nb, mw=mw, fbw=fbw, bi2=bi2, wf_load=wf_load, first=first):
                        if first:
                            e.wait_ge(self.qA.h, wf_load)
                            e.wait_ge(self.sV.h, t1T_done)
                        self.bank_wait_free(e, bi2)
                        for k in range(16):
                            inst = e.matmul(self._bank(bi2)[0:mw, :],
                                            wfs[:, k * fbw + m * 128: k * fbw + m * 128 + mw],
                                            t1T[:, k * S + nb * 512: k * S + (nb + 1) * 512],
                                            start=(k == 0), stop=(k == 15))
                        self.sPE.inc(inst)
                    mmv = self.sPE.v
                    mm_done_for_slot[slot] = mmv

                    if ftile < 4096:
                        @self.sect("scalar")
                        def _(e, nb=nb, bi2=bi2, mmv=mmv, rsq=row_store_q):
                            e.wait_ge(self.sPE.h, mmv)
                            if nb == 0:
                                e.wait_ge(self.qA.h, rsq)
                            e.activation(row[:, nb * 512:(nb + 1) * 512],
                                                      self._bank(bi2)[:], AF.Silu)
                            self.bank_set_read(bi2, self.sSC, self.sSC.v)
                        if nb == 3:
                            ev = self.sSC.v

                            @self.sect("scalar")
                            def _(e, ftile=ftile, ev=ev):
                                e.wait_ge(self.sSC.h, ev)
                                self.qA.inc(e.dma_start(self.ZS[ftile:ftile + 128, :], row[:]), 16)
                            row_store_q = self.qA.v
                    elif ftile < 8320:
                        @self.sect("vector")
                        def _(e, nb=nb, bi2=bi2, mmv=mmv):
                            e.wait_ge(self.sPE.h, mmv)
                            e.tensor_copy(xpad[:, 3 + nb * 512: 3 + (nb + 1) * 512],
                                                      self._bank(bi2)[:])
                            self.bank_set_read(bi2, self.sV, self.sV.v)
                        if nb == 3:
                            xt_i = (ftile - 4096) // 128
                            cpv = self.sV.v

                            @self.sect("vector")
                            def _(e, xt_i=xt_i, cpv=cpv):
                                e.wait_ge(self.sV.h, cpv)
                                if xt_i == 0:
                                    e.memset(xpad[:, 0:3], 0.0)
                                e.tensor_scalar(acc0[:], xpad[:, 0:S], cw[:, xt_i * 4: xt_i * 4 + 1], None, ALU.mult)
                                e.scalar_tensor_tensor(acc1[:], xpad[:, 1:1 + S], cw[:, xt_i * 4 + 1: xt_i * 4 + 2], acc0[:], ALU.mult, ALU.add)
                                e.scalar_tensor_tensor(acc0[:], xpad[:, 2:2 + S], cw[:, xt_i * 4 + 2: xt_i * 4 + 3], acc1[:], ALU.mult, ALU.add)
                                e.scalar_tensor_tensor(acc1[:], xpad[:, 3:3 + S], cw[:, xt_i * 4 + 3: xt_i * 4 + 4], acc0[:], ALU.mult, ALU.add)
                            convv = self.sV.v

                            @self.sect("scalar")
                            def _(e, xt_i=xt_i, convv=convv, rsq=row_store_q):
                                e.wait_ge(self.sV.h, convv)
                                if xt_i < 32:
                                    e.wait_ge(self.qA.h, rsq)
                                    dst = row[:]
                                else:
                                    dst = self.bct[:]
                                e.activation(dst, acc1[:], AF.Silu, bias=cb[:, xt_i:xt_i + 1])
                            sv = self.sSC.v
                            if xt_i < 32:
                                @self.sect("scalar")
                                def _(e, xt_i=xt_i, sv=sv):
                                    e.wait_ge(self.sSC.h, sv)
                                    self.qA.inc(e.dma_start(self.XT[xt_i * 128:(xt_i + 1) * 128, :], row[:]), 16)
                                row_store_q = self.qA.v
                            else:
                                self.bct_done = sv
                    else:
                        @self.sect("vector")
                        def _(e, nb=nb, bi2=bi2, mmv=mmv):
                            e.wait_ge(self.sPE.h, mmv)
                            e.tensor_copy(dtraw[:, nb * 512:(nb + 1) * 512], self._bank(bi2)[0:32, :])
                            self.bank_set_read(bi2, self.sV, self.sV.v)
                        if nb == 3:
                            self.dtraw_done = self.sV.v
                self.pump(1)
            if fbi + 2 < len(fblocks):
                loads[fbi + 2] = emit_load(fbi + 2)
        self.xt_zs_store = self.qA.v

        @self.sect("vector")
        def _(e):
            e.wait_ge(self.sV.h, self.dtraw_done)
            e.wait_ge(self.sSC.h, expA_done)
            e.tensor_scalar(dtraw[:], dtraw[:], smallv[:, 0:1], None, ALU.add)
        t1v = self.sV.v

        @self.sect("scalar")
        def _(e):
            e.wait_ge(self.sV.h, t1v)
            e.activation(self.dt_sp[:], dtraw[:], AF.Abs)
            e.activation(self.dt_sp[:], self.dt_sp[:], AF.Exp, scale=self.c_f[0:32, 2:3])
            e.activation(self.dt_sp[:], self.dt_sp[:], AF.Ln, bias=self.c_f[0:32, 3:4])
        t2v = self.sSC.v

        @self.sect("vector")
        def _(e):
            e.wait_ge(self.sSC.h, t2v)
            e.tensor_scalar(self.A_dt[:], dtraw[:], 0.0, None, ALU.max)
            e.tensor_tensor(self.dt_sp[:], self.dt_sp[:], self.A_dt[:], ALU.add)
            e.tensor_scalar(self.A_dt[:], self.dt_sp[:], smallv[:, 2:3], None, ALU.mult)
            e.tensor_copy(dthb[:], self.dt_sp[:])
            e.tensor_tensor(dtlb[:], self.dt_sp[:], dthb[:], ALU.subtract)
        self.dt_done = self.sV.v

        for i in range(NS):
            self._pe_transpose((dthb[:, i * 128:(i + 1) * 128], dtlb[:, i * 128:(i + 1) * 128]), 128,
                               self.dtT[:, i * 32:(i + 1) * 32], dst_is_add_pair=True,
                               first_waits=[(self.sV, self.dt_done)] if i == 0 else None)
        self.dtT_done = self.sV.v

        if self.dbg == "zs":
            self._dump_scratch(self.ZS)
        if self.dbg == "xt":
            self._dump_scratch(self.XT)
        if self.dbg == "dtv":
            @self.sect("sync")
            def _(e):
                e.wait_ge(self.sV.h, self.dt_done)
                self.qS.inc(e.dma_start(self.dbg_t, self.dt_sp[:]), 16)
            self.out_done = [(self.qS, self.qS.v)]

    def _dump_scratch(self, T):
        @self.sect("sync")
        def _(e):
            e.wait_ge(self.qA.h, self.qA.v)
            self.qS.inc(e.dma_start(self.dbg_t, T), 16)
        self.out_done = [(self.qS, self.qS.v)]

    def _ssd(self, pctx):
        nc = self.nc
        E = pctx.enter_context
        xtc = E(nc.sbuf_tensor([128, 32 * CH], BF16))
        xs = E(nc.sbuf_tensor([128, 2 * DIN], BF16))
        zsc = E(nc.sbuf_tensor([128, 32 * CH], BF16))
        acs = E(nc.sbuf_tensor([32, CH], F32))
        acsT = E(nc.sbuf_tensor([128, 2 * 32], F32))
        ecsP = E(nc.sbuf_tensor([128, 32], F32))
        decT = E(nc.sbuf_tensor([128, 2 * 32], F32))
        selq = E(nc.sbuf_tensor([32, 2 * 2 * CH], BF16))
        dq = E(nc.sbuf_tensor([128, 2 * 2 * CH], F32))
        eq = E(nc.sbuf_tensor([128, 2 * 2 * CH], F32))
        Mq = E(nc.sbuf_tensor([128, 2 * 2 * CH], BF16))
        xsc = E(nc.sbuf_tensor([128, 2 * DIN], BF16))
        btok = E(nc.sbuf_tensor([128, 2 * NST], BF16))
        cc0 = E(nc.sbuf_tensor([64, CH], BF16))
        Ce = E(nc.sbuf_tensor([64, 2 * CH], BF16))
        eacs = E(nc.sbuf_tensor([64, 2 * CH], F32))
        Gs = E(nc.sbuf_tensor([128, 2 * CH], F32))
        Gm = E(nc.sbuf_tensor([128, 2 * CH], F32))
        acshl = E(nc.sbuf_tensor([32, 2 * CH], BF16))
        Sst = E(nc.sbuf_tensor([64, HEADS * HD], F32))
        Sbf = E(nc.sbuf_tensor([64, HEADS * HD], BF16))
        yg = E(nc.sbuf_tensor([128, 32 * CH], F32))
        ygb = E(nc.sbuf_tensor([128, 32 * CH], BF16))
        sq = ygb
        rr = E(nc.sbuf_tensor([1, 2 * CH], F32))
        rstd_hi = E(nc.sbuf_tensor([1, CH], BF16))
        rstd_lo = E(nc.sbuf_tensor([1, CH], BF16))
        rbc = E(nc.sbuf_tensor([128, CH], F32))

        @self.sect("vector")
        def _(e):
            e.wait_ge(self.qA.h, self.xt_zs_store)
            e.memset(Sst[:], 0.0)
            e.memset(Sbf[:], 0.0)
        sbf_ready = [self.sV.v] * HEADS

        self.yg_store = 0
        prev_chunk_pe = 0
        for c in range(NCH):
            all_v_now = self.sV.v

            @self.sect("sync")
            def _(e, c=c, all_v_now=all_v_now):
                e.wait_ge(self.sV.h, all_v_now if c >= 1 else self.dtT_done)
                e.wait_ge(self.qA.h, self.xt_zs_store)
                src = self.XT[:, c * CH:(c + 1) * CH].rearrange("(k p) j -> p k j", p=128)
                self.qS.inc(e.dma_start(xtc.ap().rearrange("p (k j) -> p k j", k=32), src), 16)
                src2 = self.ZS[:, c * CH:(c + 1) * CH].rearrange("(k p) j -> p k j", p=128)
                self.qS.inc(e.dma_start(zsc.ap().rearrange("p (k j) -> p k j", k=32), src2), 16)
                for qh in range(2):
                    self.qS.inc(e.dma_start(xs[:, qh * DIN:(qh + 1) * DIN],
                                            self.XT[:, c * CH + qh * 128: c * CH + (qh + 1) * 128],
                                            transpose=True), 16)
            ld_v = self.qS.v

            @self.sect("vector")
            def _(e, c=c, prev_chunk_pe=prev_chunk_pe):
                e.wait_ge(self.sV.h, self.dt_done)
                if c >= 1:
                    e.wait_ge(self.sPE.h, prev_chunk_pe)
                e.tensor_tensor_scan(acs[:], self.ones_a[:], self.A_dt[:, c * CH:(c + 1) * CH], 0.0, ALU.mult, ALU.add)
                e.tensor_copy(acshl[:, 0:CH], acs[:])
                e.tensor_tensor(acshl[:, CH:2 * CH], acs[:], acshl[:, 0:CH], ALU.subtract)
            acs_v = self.sV.v

            for qh in range(2):
                self._pe_transpose((acshl[:, qh * 128:(qh + 1) * 128], acshl[:, CH + qh * 128: CH + (qh + 1) * 128]), 128,
                                   acsT[:, qh * 32:(qh + 1) * 32],
                                   dst_is_add_pair=True,
                                   first_waits=[(self.sV, acs_v)])
            acsT_v = self.sV.v

            @self.sect("vector")
            def _(e):
                e.tensor_scalar(self.eyeD[:], self.c_eye32[:], acs[:, CH - 1:CH], None, ALU.mult)
                e.tensor_copy(self.csel_hi[:], self.eyeD[:])
                e.tensor_tensor(self.eyeD[:], self.eyeD[:], self.csel_hi[:], ALU.subtract)
                e.tensor_copy(self.csel_lo[:], self.eyeD[:])
            csel_v = self.sV.v
            bank0 = self._bank(0)

            @self.sect("tensor")
            def _(e, csel_v=csel_v):
                e.wait_ge(self.sV.h, csel_v)
                self.bank_wait_free(e, 0)
                e.matmul(bank0[:, 0:32], self.c_ones32[:], self.csel_hi[:], start=True, stop=False)
                self.sPE.inc(e.matmul(bank0[:, 0:32], self.c_ones32[:], self.csel_lo[:], start=False, stop=True))
            csmm_v = self.sPE.v

            @self.sect("scalar")
            def _(e, csmm_v=csmm_v):
                e.wait_ge(self.sPE.h, csmm_v)
                e.activation(ecsP[:], bank0[:, 0:32], AF.Exp)
            ecs_v = self.sSC.v

            @self.sect("vector")
            def _(e, csmm_v=csmm_v):
                e.wait_ge(self.sPE.h, csmm_v)
                e.tensor_tensor(decT[:, 0:32], bank0[:, 0:32], acsT[:, 0:32], ALU.subtract)
                e.tensor_tensor(decT[:, 32:64], bank0[:, 0:32], acsT[:, 32:64], ALU.subtract)
                self.bank_set_read(0, self.sV, self.sV.v)
                self.bank_set_read(0, self.sSC, ecs_v)
            dsub_v = self.sV.v

            @self.sect("scalar")
            def _(e, dsub_v=dsub_v):
                e.wait_ge(self.sV.h, dsub_v)
                e.activation(decT[:], decT[:], AF.Exp)
            dec_v = self.sSC.v

            @self.sect("vector")
            def _(e, c=c):
                e.wait_ge(self.sSC.h, self.bct_done)
                e.tensor_copy(cc0[:], self.bct[64:128, c * CH:(c + 1) * CH])
            cc0_v = self.sV.v
            for qh in range(2):
                self._pe_transpose(self.bct[0:64, c * CH + qh * 128: c * CH + (qh + 1) * 128], 128,
                                   btok[:, qh * NST:(qh + 1) * NST],
                                   first_waits=[(self.sSC, self.bct_done)])
            btok_v = self.sV.v

            @self.sect("tensor")
            def _(e, c=c, cc0_v=cc0_v):
                e.wait_ge(self.sV.h, cc0_v)
                self.bank_wait_free(e, 1)
                for qh in range(2):
                    self.sPE.inc(e.matmul(self._bank(1)[:, qh * 256:(qh + 1) * 256],
                                          self.bct[0:64, c * CH + qh * 128: c * CH + (qh + 1) * 128],
                                          cc0[:], start=True, stop=True))
            g_v = self.sPE.v

            @self.sect("vector")
            def _(e, g_v=g_v):
                e.wait_ge(self.sPE.h, g_v)
                e.tensor_copy(Gs[:], self._bank(1)[:])
                self.bank_set_read(1, self.sV, self.sV.v)
            gs_v = self.sV.v

            @self.sect("vector")
            def _(e, c=c, ld_v=ld_v, gs_v=gs_v):
                e.wait_ge(self.qS.h, ld_v)
                for qh in range(2):
                    in0 = xs[:, qh * DIN:(qh + 1) * DIN].rearrange("p (h j) -> p h j", j=128)
                    in1 = self.dtT[:, (2 * c + qh) * 32:(2 * c + qh) * 32 + 32].unsqueeze(2).broadcast_to([128, 32, 128])
                    out = xsc[:, qh * DIN:(qh + 1) * DIN].rearrange("p (h j) -> p h j", j=128)
                    e.tensor_tensor(out, in0, in1, ALU.mult)
                e.tensor_tensor(Gm[:], Gs[:], self.c_mask[:], ALU.mult)
            xsc_v = self.sV.v

            bc_v = [0] * (HEADS + 2)
            d_v = [0] * (HEADS + 1)
            e_v = [0] * (HEADS + 1)
            M_v = [0] * HEADS
            y_v = [0] * HEADS
            smm_v = [0] * HEADS

            def em_sel_bc(h):
                par = h % 2
                bc_b = 1 + par
                bcb = self._bank(bc_b)

                @self.sect("vector")
                def _(e, h=h, par=par):
                    if h >= 2:
                        e.wait_ge(self.sPE.h, bc_v[h - 2])
                    e.tensor_scalar(selq[:, par * 512: par * 512 + 2 * CH],
                                    acshl[:], self.c_eye32[:, h:h + 1], None, ALU.mult)
                sel = self.sV.v

                @self.sect("tensor")
                def _(e, h=h, par=par, sel=sel, bc_b=bc_b, bcb=bcb):
                    e.wait_ge(self.sV.h, sel)
                    self.bank_wait_free(e, bc_b)
                    e.matmul(bcb[:, 0:CH], self.c_ones32[:], selq[:, par * 512: par * 512 + CH], start=True, stop=False)
                    self.sPE.inc(e.matmul(bcb[:, 0:CH], self.c_ones32[:], selq[:, par * 512 + CH: par * 512 + 2 * CH], start=False, stop=True))
                bc_v[h] = self.sPE.v

            def em_de(h):
                par = h % 2
                bcb = self._bank(1 + par)

                @self.sect("vector")
                def _(e, h=h, par=par):
                    e.wait_ge(self.sPE.h, bc_v[h])
                    if h >= 2:
                        e.wait_ge(self.sSC.h, e_v[h - 2])
                    for qh in range(2):
                        e.tensor_scalar(dq[:, par * 512 + qh * CH: par * 512 + (qh + 1) * CH], bcb[:, 0:CH],
                                        acsT[:, qh * 32 + h: qh * 32 + h + 1], 0.0, ALU.subtract, ALU.min)
                d_v[h] = self.sV.v

                @self.sect("scalar")
                def _(e, h=h, par=par):
                    e.wait_ge(self.sV.h, d_v[h])
                    if h >= 2:
                        e.wait_ge(self.sV.h, M_v[h - 2])
                    e.activation(eq[:, par * 512: par * 512 + 2 * CH], dq[:, par * 512: par * 512 + 2 * CH], AF.Exp)
                    e.activation(eacs[:, par * CH:(par + 1) * CH], bcb[0:64, 0:CH], AF.Exp)
                    self.bank_set_read(1 + par, self.sSC, self.sSC.v)
                e_v[h] = self.sSC.v

            def em_head(h):
                par = h % 2
                y_b = 3 + par
                yb = self._bank(y_b)
                s_b = 5 if par == 0 else 0

                @self.sect("vector")
                def _(e, h=h, par=par):
                    e.wait_ge(self.sSC.h, e_v[h])
                    if h >= 2:
                        e.wait_ge(self.sPE.h, y_v[h - 2])
                    e.tensor_tensor(Mq[:, par * 512: par * 512 + 2 * CH], eq[:, par * 512: par * 512 + 2 * CH],
                                    Gm[:], ALU.mult)
                    e.tensor_tensor(Ce[:, par * CH:(par + 1) * CH], eacs[:, par * CH:(par + 1) * CH], cc0[:], ALU.mult)
                M_v[h] = self.sV.v

                @self.sect("tensor")
                def _(e, h=h, par=par, y_b=y_b, yb=yb):
                    e.wait_ge(self.sV.h, M_v[h])
                    e.wait_ge(self.sV.h, sbf_ready[h])
                    self.bank_wait_free(e, y_b)
                    e.matmul(yb[:, 0:CH], xsc[:, h * HD:(h + 1) * HD], Mq[:, par * 512: par * 512 + CH], start=True, stop=False)
                    e.matmul(yb[:, 0:CH], xsc[:, DIN + h * HD: DIN + (h + 1) * HD], Mq[:, par * 512 + CH: par * 512 + 2 * CH], start=False, stop=False)
                    self.sPE.inc(e.matmul(yb[:, 0:CH], Sbf[:, h * HD:(h + 1) * HD], Ce[:, par * CH:(par + 1) * CH], start=False, stop=True))
                y_v[h] = self.sPE.v

                @self.sect("vector")
                def _(e, h=h, yb=yb, y_b=y_b):
                    e.wait_ge(self.sPE.h, y_v[h])
                    e.scalar_tensor_tensor(self.ytmp[:], xtc[:, h * CH:(h + 1) * CH],
                                           self.Dcol[:, h:h + 1], yb[:, 0:CH], ALU.mult, ALU.add)
                    e.tensor_tensor(yg[:, h * CH:(h + 1) * CH], self.ytmp[:],
                                    zsc[:, h * CH:(h + 1) * CH], ALU.mult)
                    self.bank_set_read(y_b, self.sV, self.sV.v)

                @self.sect("vector")
                def _(e, h=h, par=par, dec_v=dec_v):
                    e.wait_ge(self.sSC.h, dec_v)
                    if h >= 2:
                        e.wait_ge(self.sPE.h, smm_v[h - 2])
                    for qh in range(2):
                        e.tensor_scalar(self.bd[:, par * 2 * NST + qh * NST: par * 2 * NST + (qh + 1) * NST],
                                        btok[:, qh * NST:(qh + 1) * NST],
                                        decT[:, qh * 32 + h: qh * 32 + h + 1], None, ALU.mult)
                bd_v = self.sV.v

                @self.sect("tensor")
                def _(e, h=h, par=par, bd_v=bd_v, s_b=s_b):
                    e.wait_ge(self.sV.h, bd_v)
                    self.bank_wait_free(e, s_b)
                    e.matmul(self._bank(s_b)[0:64, 0:HD], self.bd[:, par * 2 * NST: par * 2 * NST + NST],
                             xsc[:, h * HD:(h + 1) * HD], start=True, stop=False)
                    self.sPE.inc(e.matmul(self._bank(s_b)[0:64, 0:HD], self.bd[:, par * 2 * NST + NST: par * 2 * NST + 2 * NST],
                                          xsc[:, DIN + h * HD: DIN + (h + 1) * HD], start=False, stop=True))
                smm_v[h] = self.sPE.v

                @self.sect("vector")
                def _(e, h=h, s_b=s_b, ecs_v=ecs_v):
                    e.wait_ge(self.sPE.h, smm_v[h])
                    e.wait_ge(self.sSC.h, ecs_v)
                    e.scalar_tensor_tensor(Sst[:, h * HD:(h + 1) * HD], Sst[:, h * HD:(h + 1) * HD],
                                           ecsP[0:64, h:h + 1], self._bank(s_b)[0:64, 0:HD], ALU.mult, ALU.add)
                    e.tensor_copy(Sbf[:, h * HD:(h + 1) * HD], Sst[:, h * HD:(h + 1) * HD])
                    self.bank_set_read(s_b, self.sV, self.sV.v)
                    sbf_ready[h] = self.sV.v

            em_sel_bc(0)
            em_sel_bc(1)
            em_de(0)
            for h in range(HEADS):
                if h + 2 < HEADS:
                    em_sel_bc(h + 2)
                if h + 1 < HEADS:
                    em_de(h + 1)
                em_head(h)
            prev_chunk_pe = self.sPE.v
            self.pump(2)

            ygv = self.sV.v

            @self.sect("vector")
            def _(e, ygv=ygv):
                e.wait_ge(self.sV.h, ygv)
                e.tensor_tensor(sq[:], yg[:], yg[:], ALU.mult)
            sq_v = self.sV.v

            @self.sect("tensor")
            def _(e, sq_v=sq_v):
                e.wait_ge(self.sV.h, sq_v)
                e.wait_ge(self.sV.h, self.consts_v)
                self.bank_wait_free(e, 0)
                for k in range(32):
                    inst = e.matmul(self._bank(0)[0:1, 0:CH], self.c_onesc[:],
                                    sq[:, k * CH:(k + 1) * CH], start=(k == 0), stop=(k == 31))
                self.sPE.inc(inst)
            sqmm_v = self.sPE.v

            @self.sect("scalar")
            def _(e, sqmm_v=sqmm_v):
                e.wait_ge(self.sPE.h, sqmm_v)
                e.activation(rr[0:1, 0:CH], self._bank(0)[0:1, 0:CH], AF.Ln, scale=self.c_f[0:1, 4:5], bias=self.c_f[0:1, 0:1])
                e.activation(rr[0:1, CH:2 * CH], rr[0:1, 0:CH], AF.Exp, scale=self.c_f[0:1, 1:2])
                self.bank_set_read(0, self.sSC, self.sSC.v)
            rstd_v = self.sSC.v

            @self.sect("vector")
            def _(e, rstd_v=rstd_v):
                e.wait_ge(self.sSC.h, rstd_v)
                e.tensor_copy(rstd_hi[:], rr[0:1, CH:2 * CH])
                e.tensor_tensor(rstd_lo[:], rr[0:1, CH:2 * CH], rstd_hi[:], ALU.subtract)
            rsp_v = self.sV.v

            @self.sect("tensor")
            def _(e, rsp_v=rsp_v):
                e.wait_ge(self.sV.h, rsp_v)
                self.bank_wait_free(e, 0)
                e.matmul(self._bank(0)[:, 0:CH], self.c_ones1[:], rstd_hi[:], start=True, stop=False)
                self.sPE.inc(e.matmul(self._bank(0)[:, 0:CH], self.c_ones1[:], rstd_lo[:], start=False, stop=True))
            rbc_v = self.sPE.v

            @self.sect("vector")
            def _(e, rbc_v=rbc_v, c=c, ygst=self.yg_store):
                e.wait_ge(self.sPE.h, rbc_v)
                if c >= 1:
                    e.wait_ge(self.qA.h, ygst)
                e.tensor_copy(rbc[:], self._bank(0)[:, 0:CH])
                for k in range(32):
                    e.scalar_tensor_tensor(ygb[:, k * CH:(k + 1) * CH], yg[:, k * CH:(k + 1) * CH],
                                           self.nw[:, k:k + 1], rbc[:], ALU.mult, ALU.mult)
                self.bank_set_read(0, self.sV, self.sV.v)
            ygb_v = self.sV.v

            @self.sect("scalar")
            def _(e, ygb_v=ygb_v, c=c):
                e.wait_ge(self.sV.h, ygb_v)
                dst = self.YG[:, c * CH:(c + 1) * CH].rearrange("(k p) j -> p k j", p=128)
                self.qA.inc(e.dma_start(dst, ygb.ap().rearrange("p (k j) -> p k j", k=32)), 16)
            self.yg_store = self.qA.v

        if self.dbg == "yg":
            self._dump_scratch(self.YG)

    def _w_out_pass(self):
        nc = self.nc
        with ExitStack() as pctx:
            E = pctx.enter_context
            wo = E(nc.sbuf_tensor([128, 32 * D], BF16))
            lhs = E(nc.sbuf_tensor([128, 2 * 32 * 128], BF16))
            x0t = E(nc.sbuf_tensor([128, 2 * D], F32))
            xo = E(nc.sbuf_tensor([128, 2 * D], F32))
            self.x_store = [None] * NS

            @self.sect("sync")
            def _(e):
                e.wait_ge(self.qG.h, self.cast_done["W_out"])
                e.wait_ge(self.qA.h, self.yg_store)
                src = self.W_outB.rearrange("(k p) j -> p k j", p=128)
                self.qS.inc(e.dma_start(wo.ap().rearrange("p (k j) -> p k j", k=32), src), 16)
            wo_v = self.qS.v

            lhs_used = [0, 0]
            x0_used = [0, 0]
            xo_stored = [0, 0]
            for i in range(NS):
                slot = i % 2
                lh = lhs[:, slot * 32 * 128:(slot + 1) * 32 * 128]

                @self.sect("sync")
                def _(e, i=i, slot=slot, lh=lh):
                    if i < 2:
                        e.wait_ge(self.qA.h, self.yg_store)
                    else:
                        e.wait_ge(self.sPE.h, lhs_used[slot])
                    src = self.YG[:, i * 128:(i + 1) * 128].rearrange("(k p) j -> p k j", p=128)
                    self.qS.inc(e.dma_start(lh.rearrange("p (k j) -> p k j", k=32), src), 16)
                lh_v = self.qS.v

                @self.sect("sync")
                def _(e, i=i, slot=slot):
                    if i >= 2:
                        e.wait_ge(self.sV.h, x0_used[slot])
                    else:
                        e.wait_ge(self.qA.h, self.yg_store)
                    self.qS.inc(e.dma_start(x0t[:, slot * D:(slot + 1) * D],
                                            self.u[i * 128:(i + 1) * 128, :]), 16)
                x0_v = self.qS.v

                xos = xo[:, slot * D:(slot + 1) * D]
                for nb in range(4):
                    bi = nb

                    @self.sect("tensor")
                    def _(e, lh=lh, nb=nb, bi=bi, lh_v=lh_v, wo_v=wo_v, first=(i == 0 and nb == 0)):
                        if first:
                            e.wait_ge(self.qS.h, wo_v)
                        e.wait_ge(self.qS.h, lh_v)
                        self.bank_wait_free(e, bi)
                        for k in range(32):
                            inst = e.matmul(self._bank(bi)[:], lh[:, k * 128:(k + 1) * 128],
                                            wo[:, k * D + nb * 512: k * D + (nb + 1) * 512],
                                            start=(k == 0), stop=(k == 31))
                        self.sPE.inc(inst)
                        lhs_used[slot] = self.sPE.v
                    mm_v = self.sPE.v

                    @self.sect("vector")
                    def _(e, nb=nb, bi=bi, mm_v=mm_v, x0_v=x0_v, slot=slot, xos=xos):
                        e.wait_ge(self.sPE.h, mm_v)
                        e.wait_ge(self.qS.h, x0_v)
                        if nb == 0:
                            e.wait_ge(self.qA.h, xo_stored[slot])
                        e.tensor_tensor(xos[:, nb * 512:(nb + 1) * 512],
                                                    self._bank(bi)[:], x0t[:, slot * D + nb * 512: slot * D + (nb + 1) * 512], ALU.add)
                        self.bank_set_read(bi, self.sV, self.sV.v)
                        x0_used[slot] = self.sV.v
                ev_v = self.sV.v

                @self.sect("scalar")
                def _(e, i=i, ev_v=ev_v, xos=xos):
                    e.wait_ge(self.sV.h, ev_v)
                    self.qA.inc(e.dma_start(self.X1[i * 128:(i + 1) * 128, :], xos), 16)
                xo_stored[slot] = self.qA.v
                self.x_store[i] = (self.qA, self.qA.v)
                self.pump(1)

        if self.dbg == "x1":
            @self.sect("sync")
            def _(e):
                for (ctr, v) in self.x_store:
                    e.wait_ge(ctr.h, v)
                self.qS.inc(e.dma_start(self.dbg_t, self.X1), 16)
            self.out_done = [(self.qS, self.qS.v)]

    def _gate_prep(self):
        nc = self.nc
        with ExitStack() as p:
            E = p.enter_context
            gwf = E(nc.sbuf_tensor([16, D], F32))
            gwh = E(nc.sbuf_tensor([16, D], BF16))
            gwl = E(nc.sbuf_tensor([16, D], BF16))

            @self.sect("sync")
            def _(e):
                for (ctr, v) in self.x_store:
                    e.wait_ge(ctr.h, v)
                self.qS.inc(e.dma_start(gwf[:], self.gate_w), 16)
            gl = self.qS.v

            @self.sect("vector")
            def _(e):
                e.wait_ge(self.qS.h, gl)
                e.tensor_copy(gwh[:], gwf[:])
                e.tensor_tensor(gwl[:], gwf[:], gwh[:], ALU.subtract)
            spl = self.sV.v
            for k in range(16):
                self._pe_transpose(gwh[:, k * 128:(k + 1) * 128], 128,
                                   self.gwT_hi[:, k * 16:(k + 1) * 16],
                                   first_waits=[(self.sV, spl)] if k == 0 else None)
                self._pe_transpose(gwl[:, k * 128:(k + 1) * 128], 128,
                                   self.gwT_lo[:, k * 16:(k + 1) * 16])
        self.gate_prep_done = self.sV.v

    def _ffn(self, Xin, Xout, xin_ready, call):
        nc = self.nc
        HS = S // 2
        with ExitStack() as p:
            E = p.enter_context
            self.wg = E(nc.sbuf_tensor([128, NS * 16], F32))
            t2T = E(nc.sbuf_tensor([128, NS * S], BF16))
            with ExitStack() as pn:
                self._norm_t(pn, Xin, t2T, tile_ready=xin_ready, gate=True)
            t2T_done = self.sV.v

            y = E(nc.sbuf_tensor([128, 8 * D], BF16))
            h = E(nc.sbuf_tensor([128, 11 * HS], BF16))
            wb1 = E(nc.sbuf_tensor([128, 2 * 16 * 128], BF16))
            wb3 = E(nc.sbuf_tensor([128, 2 * 16 * 128], BF16))
            w2c = E(nc.sbuf_tensor([128, 2 * 11 * 512], BF16))
            stmp = E(nc.sbuf_tensor([128, 2 * 512], F32))
            xin_t = E(nc.sbuf_tensor([128, D], F32))
            xo = E(nc.sbuf_tensor([128, D], F32))

            experts = []
            for ei in range(NEXP):
                experts.append((f"w1.{ei}", self.w1t[ei], f"w3.{ei}", self.w3t[ei],
                                f"w2.{ei}", self.w2t[ei], ei, 0, 0))
            experts.append(("sw1", self.sw1t, "sw3", self.sw3t, "sw2", self.sw2t, None, 0, 0))
            experts.append(("sw1", self.sw1t, "sw3", self.sw3t, "sw2", self.sw2t, None, 11, 11))

            self.x_store = [None] * NS
            wslot_used = [0, 0]
            w2slot_used = [0, 0]
            stmp_used = [0, 0]
            xin_used = [0, 0]
            xo_stored = [0, 0]
            for hh in range(2):
                y_first = True
                for xi, (n1, a1, n3, a3, n2, a2, ei, ib, k2b) in enumerate(experts):
                    ipairs = [(i, i + 1) for i in range(11)]
                    tpv1 = max(self.tp_done_get(n1), self.tp_done_get(n3))
                    for pi, (i0, i1) in enumerate(ipairs):
                        jw = (i1 - i0) * 128
                        slot = pi % 2
                        wb1s = wb1[:, slot * 16 * 128: slot * 16 * 128 + 16 * jw]
                        wb3s = wb3[:, slot * 16 * 128: slot * 16 * 128 + 16 * jw]

                        @self.sect("scalar")
                        def _(e, a1=a1, a3=a3, i0=i0, slot=slot, ib=ib,
                              wb1s=wb1s, wb3s=wb3s, tpv1=tpv1):
                            e.wait_ge(self.qG.h, tpv1)
                            e.wait_ge(self.sPE.h, wslot_used[slot])
                            self.qA.inc(e.dma_start(wb1s.rearrange("p (k j) -> p k j", k=16), a1[ib + i0]), 16)
                            self.qA.inc(e.dma_start(wb3s.rearrange("p (k j) -> p k j", k=16), a3[ib + i0]), 16)
                        wl_v = self.qA.v

                        for i in range(i0, i1):
                            ioff = (i - i0) * 128
                            for nb in range(2):
                                b1 = (2 * i + nb) % 2
                                b3 = 2 + (2 * i + nb) % 2

                                @self.sect("tensor")
                                def _(e, i=i, ioff=ioff, nb=nb, b1=b1, b3=b3, jw=jw, hh=hh,
                                      wb1s=wb1s, wb3s=wb3s, wl_v=wl_v,
                                      first=(xi == 0 and i == 0 and nb == 0)):
                                    if first:
                                        e.wait_ge(self.sV.h, t2T_done)
                                    e.wait_ge(self.qA.h, wl_v)
                                    self.bank_wait_free(e, b1)
                                    for k in range(16):
                                        inst = e.matmul(self._bank(b1)[:],
                                                        wb1s[:, k * jw + ioff: k * jw + ioff + 128],
                                                        t2T[:, k * S + hh * HS + nb * 512: k * S + hh * HS + (nb + 1) * 512],
                                                        start=(k == 0), stop=(k == 15))
                                    self.sPE.inc(inst)
                                    self.bank_wait_free(e, b3)
                                    for k in range(16):
                                        inst = e.matmul(self._bank(b3)[:],
                                                        wb3s[:, k * jw + ioff: k * jw + ioff + 128],
                                                        t2T[:, k * S + hh * HS + nb * 512: k * S + hh * HS + (nb + 1) * 512],
                                                        start=(k == 0), stop=(k == 15))
                                    self.sPE.inc(inst)
                                    wslot_used[slot] = self.sPE.v
                                mm_v = self.sPE.v

                                sslot = (2 * i + nb) % 2

                                @self.sect("scalar")
                                def _(e, b1=b1, mm_v=mm_v, sslot=sslot):
                                    e.wait_ge(self.sPE.h, mm_v)
                                    e.wait_ge(self.sV.h, stmp_used[sslot])
                                    e.activation(stmp[:, sslot * 512:(sslot + 1) * 512],
                                                 self._bank(b1)[:], AF.Silu)
                                    self.bank_set_read(b1, self.sSC, self.sSC.v)
                                sil_v = self.sSC.v

                                @self.sect("vector")
                                def _(e, b3=b3, sil_v=sil_v, sslot=sslot, i=i, nb=nb):
                                    e.wait_ge(self.sSC.h, sil_v)
                                    e.tensor_tensor(h[:, i * HS + nb * 512: i * HS + (nb + 1) * 512],
                                                    stmp[:, sslot * 512:(sslot + 1) * 512],
                                                    self._bank(b3)[:], ALU.mult)
                                    self.bank_set_read(b3, self.sV, self.sV.v)
                                    stmp_used[sslot] = self.sV.v
                    h_done = self.sV.v

                    tpv2 = self.tp_done_get(n2)
                    for nb in range(4):
                        slot = nb % 2
                        w2s = w2c[:, slot * 11 * 512:(slot + 1) * 11 * 512]

                        @self.sect("scalar")
                        def _(e, a2=a2, nb=nb, w2s=w2s, slot=slot, tpv2=tpv2, k2b=k2b):
                            e.wait_ge(self.qG.h, tpv2)
                            e.wait_ge(self.sPE.h, w2slot_used[slot])
                            src = a2[nb][:, k2b:k2b + 11, :]
                            self.qA.inc(e.dma_start(w2s.rearrange("p (k j) -> p k j", k=11), src), 16)
                        w2l_v = self.qA.v

                        for si in range(8):
                            b = 4 + si % 2

                            @self.sect("tensor")
                            def _(e, si=si, nb=nb, b=b, w2s=w2s, w2l_v=w2l_v, hh=hh,
                                  h_done=h_done, first=(nb == 0 and si == 0)):
                                if first:
                                    e.wait_ge(self.sV.h, h_done)
                                e.wait_ge(self.qA.h, w2l_v)
                                self.bank_wait_free(e, b)
                                for k in range(11):
                                    inst = e.matmul(self._bank(b)[:],
                                                    h[:, k * HS + si * 128: k * HS + si * 128 + 128],
                                                    w2s[:, k * 512:(k + 1) * 512],
                                                    start=(k == 0), stop=(k == 10))
                                self.sPE.inc(inst)
                                w2slot_used[slot] = self.sPE.v
                            mmv = self.sPE.v

                            @self.sect("vector")
                            def _(e, si=si, nb=nb, b=b, mmv=mmv, ei=ei, hh=hh, y_first=y_first):
                                e.wait_ge(self.sPE.h, mmv)
                                ysl = y[:, si * D + nb * 512: si * D + (nb + 1) * 512]
                                if ei is not None:
                                    wcol = self.wg[:, (hh * 8 + si) * 16 + ei: (hh * 8 + si) * 16 + ei + 1]
                                    if y_first:
                                        e.tensor_scalar(ysl, self._bank(b)[:], wcol, None, ALU.mult)
                                    else:
                                        e.scalar_tensor_tensor(ysl, self._bank(b)[:], wcol, ysl, ALU.mult, ALU.add)
                                else:
                                    e.tensor_tensor(ysl, ysl, self._bank(b)[:], ALU.add)
                                self.bank_set_read(b, self.sV, self.sV.v)
                    y_first = False
                    self.pump(2)

                for si in range(8):
                    i = hh * 8 + si

                    @self.sect("sync")
                    def _(e, i=i, si=si):
                        ctr, v = xin_ready[i]
                        e.wait_ge(ctr.h, v)
                        if xin_used[0]:
                            e.wait_ge(self.sV.h, xin_used[0])
                        self.qS.inc(e.dma_start(xin_t[:], Xin[i * 128:(i + 1) * 128, :]), 16)
                    xl_v = self.qS.v

                    @self.sect("vector")
                    def _(e, si=si, xl_v=xl_v):
                        e.wait_ge(self.qS.h, xl_v)
                        if xo_stored[0]:
                            e.wait_ge(self.qA.h, xo_stored[0])
                        e.tensor_tensor(xo[:], xin_t[:], y[:, si * D:(si + 1) * D], ALU.add)
                        xin_used[0] = self.sV.v
                    ev = self.sV.v

                    @self.sect("scalar")
                    def _(e, i=i, ev=ev):
                        e.wait_ge(self.sV.h, ev)
                        self.qA.inc(e.dma_start(Xout[i * 128:(i + 1) * 128, :], xo[:]), 16)
                    xo_stored[0] = self.qA.v
                    self.x_store[i] = (self.qA, self.qA.v)
        self.out_done = list(self.x_store)

    def _attnb(self):
        nc = self.nc
        with ExitStack() as p:
            E = p.enter_context
            rs_all = E(nc.sbuf_tensor([128, NS], F32))
            mb_all = E(nc.sbuf_tensor([128, NS], F32))
            t3T = E(nc.sbuf_tensor([128, NS * S], BF16))
            with ExitStack() as pn:
                self._norm_t(pn, self.X2, t3T, tile_ready=self.x_store,
                             save_stats=(rs_all, mb_all))
            t3T_done = self.sV.v

            stT = E(nc.sbuf_tensor([128, 8 * S], BF16))
            CTs = E(nc.sbuf_tensor([128, 8 * D], BF16))
            Db = E(nc.sbuf_tensor([128, D], F32))
            dhl = E(nc.sbuf_tensor([1, D], F32))
            dhb = E(nc.sbuf_tensor([1, D], BF16))
            dlb = E(nc.sbuf_tensor([1, D], BF16))
            btile = E(nc.sbuf_tensor([128, 2 * 16 * 128], BF16))
            x2t = E(nc.sbuf_tensor([128, 2 * D], F32))
            xo = E(nc.sbuf_tensor([128, D], F32))
            t3r = E(nc.sbuf_tensor([128, 512], F32))
            q1 = E(nc.sbuf_tensor([128, 512], F32))

            tpC = self.tp_done_get("C")
            tpB = self.tp_done_get("B")
            prior = list(self.x_store)

            @self.sect("sync")
            def _(e):
                for (ctr, v) in prior:
                    e.wait_ge(ctr.h, v)
                self.qS.inc(e.dma_start(dhl[0:1, 0:D], self.Dab), 16)
                src = self.Ct.rearrange("(k p) j -> p k j", p=128)
                e.wait_ge(self.qG.h, tpC)
                self.qS.inc(e.dma_start(CTs.ap().rearrange("p (k j) -> p k j", k=8), src), 16)
            dl_v = self.qS.v

            @self.sect("vector")
            def _(e):
                e.wait_ge(self.qS.h, dl_v)
                e.tensor_copy(dhb[:], dhl[0:1, 0:D])
                e.tensor_tensor(dlb[:], dhl[0:1, 0:D], dhb[:], ALU.subtract)
            dspl = self.sV.v

            for nb in range(4):
                b = 5

                @self.sect("tensor")
                def _(e, nb=nb, dspl=dspl):
                    if nb == 0:
                        e.wait_ge(self.sV.h, dspl)
                    self.bank_wait_free(e, 5)
                    e.matmul(self._bank(5)[:], self.c_ones1[:], dhb[0:1, nb * 512:(nb + 1) * 512], start=True, stop=False)
                    self.sPE.inc(e.matmul(self._bank(5)[:], self.c_ones1[:], dlb[0:1, nb * 512:(nb + 1) * 512], start=False, stop=True))
                dmm = self.sPE.v

                @self.sect("vector")
                def _(e, nb=nb, dmm=dmm):
                    e.wait_ge(self.sPE.h, dmm)
                    e.tensor_copy(Db[:, nb * 512:(nb + 1) * 512], self._bank(5)[:])
                    self.bank_set_read(5, self.sV, self.sV.v)

            bslot_used = [0, 0]
            for m in range(8):
                slot = m % 2
                bt = btile[:, slot * 16 * 128:(slot + 1) * 16 * 128]

                @self.sect("scalar")
                def _(e, m=m, bt=bt, slot=slot, first=(m < 2)):
                    if first:
                        e.wait_ge(self.qG.h, tpB)
                    e.wait_ge(self.sPE.h, bslot_used[slot])
                    src = self.Bt[:, m * 128:(m + 1) * 128].rearrange("(k p) j -> p k j", p=128)
                    self.qA.inc(e.dma_start(bt.rearrange("p (k j) -> p k j", k=16), src), 16)
                bl_v = self.qA.v

                for nb in range(4):
                    b = m % 2

                    @self.sect("tensor")
                    def _(e, m=m, nb=nb, b=b, bt=bt, bl_v=bl_v, slot=slot, first=(m == 0 and nb == 0)):
                        if first:
                            e.wait_ge(self.sV.h, t3T_done)
                        e.wait_ge(self.qA.h, bl_v)
                        self.bank_wait_free(e, b)
                        for k in range(16):
                            inst = e.matmul(self._bank(b)[:], bt[:, k * 128:(k + 1) * 128],
                                            t3T[:, k * S + nb * 512: k * S + (nb + 1) * 512],
                                            start=(k == 0), stop=(k == 15))
                        self.sPE.inc(inst)
                        bslot_used[slot] = self.sPE.v
                    mmv = self.sPE.v

                    @self.sect("scalar")
                    def _(e, m=m, nb=nb, b=b, mmv=mmv):
                        e.wait_ge(self.sPE.h, mmv)
                        e.activation(stT[:, m * S + nb * 512: m * S + (nb + 1) * 512],
                                     self._bank(b)[:], AF.Tanh)
                        self.bank_set_read(b, self.sSC, self.sSC.v)
            st_done = self.sSC.v

            x2_used = [0, 0]
            xo_stored = [0, 0]
            self.x_store = [None] * NS
            for si in range(NS):
                slot = si % 2

                @self.sect("sync")
                def _(e, si=si, slot=slot):
                    if si >= 2:
                        e.wait_ge(self.sV.h, x2_used[slot])
                    self.qS.inc(e.dma_start(x2t[:, slot * D:(slot + 1) * D],
                                            self.X2[si * 128:(si + 1) * 128, :]), 16)
                xl_v = self.qS.v

                for nb in range(4):
                    b = 2 + si % 2

                    @self.sect("tensor")
                    def _(e, si=si, nb=nb, b=b, st_done=st_done, dl_v=dl_v, first=(si == 0 and nb == 0)):
                        if first:
                            e.wait_ge(self.sSC.h, st_done)
                            e.wait_ge(self.qS.h, dl_v)
                        self.bank_wait_free(e, b)
                        for k in range(8):
                            inst = e.matmul(self._bank(b)[:],
                                            stT[:, k * S + si * 128: k * S + si * 128 + 128],
                                            CTs[:, k * D + nb * 512: k * D + (nb + 1) * 512],
                                            start=(k == 0), stop=(k == 7))
                        self.sPE.inc(inst)
                    mmv = self.sPE.v

                    @self.sect("vector")
                    def _(e, si=si, nb=nb, b=b, mmv=mmv, xl_v=xl_v, slot=slot):
                        e.wait_ge(self.sPE.h, mmv)
                        e.wait_ge(self.qS.h, xl_v)
                        if nb == 0 and xo_stored[0]:
                            e.wait_ge(self.qA.h, xo_stored[0])
                        xsl = x2t[:, slot * D + nb * 512: slot * D + (nb + 1) * 512]
                        e.tensor_scalar(t3r[:], xsl, rs_all[:, si:si + 1], mb_all[:, si:si + 1], ALU.mult, ALU.add)
                        e.tensor_tensor(q1[:], t3r[:], Db[:, nb * 512:(nb + 1) * 512], ALU.mult)
                        e.tensor_tensor(q1[:], q1[:], self._bank(b)[:], ALU.add)
                        e.tensor_tensor(xo[:, nb * 512:(nb + 1) * 512], q1[:], xsl, ALU.add)
                        self.bank_set_read(b, self.sV, self.sV.v)
                        x2_used[slot] = self.sV.v
                ev = self.sV.v

                @self.sect("scalar")
                def _(e, si=si, ev=ev):
                    e.wait_ge(self.sV.h, ev)
                    self.qA.inc(e.dma_start(self.X3[si * 128:(si + 1) * 128, :], xo[:]), 16)
                xo_stored[0] = self.qA.v
                self.x_store[si] = (self.qA, self.qA.v)

        if self.dbg == "x3":
            @self.sect("sync")
            def _(e):
                for (ctr, v) in self.x_store:
                    e.wait_ge(ctr.h, v)
                self.qS.inc(e.dma_start(self.dbg_t, self.X3), 16)
            self.out_done = [(self.qS, self.qS.v)]

    def tp_done_get(self, name):
        while self.prepass_left and name not in self.tp_done:
            next(self.prepass_gen)
        return self.tp_done[name][1]

    def _finish(self):
        if not hasattr(self, "out_done"):
            self.out_done = [(self.qA, self.qA.v)]

        @self.sect("sync")
        def _(e):
            for (ctr, v) in self.out_done:
                e.wait_ge(ctr.h, v)


def build(nc, dbg=DEBUG_OUT):
    _Builder(nc, dbg).build()
    return nc


def _mk_consts():
    idb = np.eye(128, dtype=np.float32).astype(ml_dtypes.bfloat16)
    ones32 = np.ones((32, 128), dtype=np.float32).astype(ml_dtypes.bfloat16)
    ones1 = np.ones((1, 128), dtype=np.float32).astype(ml_dtypes.bfloat16)
    eye32 = np.eye(32, dtype=np.float32)
    m = np.zeros((128, 512), dtype=np.float32)
    for qh in range(2):
        s_idx = np.arange(128)[:, None] + qh * 128
        l_idx = np.arange(256)[None, :]
        m[:, qh * 256:(qh + 1) * 256] = (s_idx <= l_idx).astype(np.float32)
    return idb, ones32, ones1, eye32, m.astype(ml_dtypes.bfloat16)


def make_inputs_for_core(b, full):
    idb, ones32, ones1, eye32, mask = _mk_consts()
    d = {
        "u": full["L0_attn_a_u"][b],
        "W_in": full["L0_attn_a_W_in"],
        "conv_w": full["L0_attn_a_conv_w"].reshape(33, 128, 4).transpose(1, 0, 2).reshape(128, 132),
        "conv_b": full["L0_attn_a_conv_b"].reshape(33, 128).T,
        "dt_bias": full["L0_attn_a_dt_bias"].reshape(32, 1),
        "A_log": full["L0_attn_a_A_log"].reshape(32, 1),
        "Dh": full["L0_attn_a_D"].reshape(32, 1),
        "norm_w": full["L0_attn_a_norm_w"].reshape(32, 128).T,
        "W_out": full["L0_attn_a_W_out"],
        "Bw": full["L0_attn_b_B"],
        "Cw": full["L0_attn_b_C"],
        "Dab": full["L0_attn_b_D"].reshape(1, 2048),
        "gate_w": full["L0_ffn_gate_w"],
        "w1": full["L0_ffn_w1"],
        "w2": full["L0_ffn_w2"],
        "w3": full["L0_ffn_w3"],
        "sw1": full["L0_ffn_sw1"],
        "sw2": full["L0_ffn_sw2"],
        "sw3": full["L0_ffn_sw3"],
        "id_bf": idb, "ones32_bf": ones32, "ones1_bf": ones1,
        "eye32_f": eye32, "mask_bf": mask,
        "f_const": np.tile(np.array([1e-5, -0.5, -1.0, 1.0, 1.0 / 4096, 0.0, 0.0, 0.0], np.float32), (128, 1)),
    }
    return {k: np.ascontiguousarray(v) for k, v in d.items()}


def kernel(**inputs):
    nc = bass.Bass("TRN2", target_bir_lowering=False)
    build(nc)
    in_maps = [make_inputs_for_core(b, inputs) for b in range(8)]
    res = run_bass_kernel_spmd(nc, in_maps, core_ids=list(range(8)))
    return np.stack([r["out"] for r in res.results], axis=0)
